# version 12
# baseline (speedup 1.0000x reference)
import contextlib
import numpy as np
import ml_dtypes
import concourse.bass as bass
import concourse.mybir as mybir
from concourse.bass_utils import run_bass_kernel_spmd

F32 = mybir.dt.float32
BF16 = mybir.dt.bfloat16
OP = mybir.AluOpType
ACT = mybir.ActivationFunctionType
AX = mybir.AxisListType.X

C = 2048
KC = 16
T = 4096
B = 4
TLOC = 2048
NH = 16
NKVH = 4
HD = 128
LKV = 512
BAND = 2560
NGLOB = 64
NKV = BAND + 2 * NGLOB
NKVC = NKV // 128
NKEY = 704
NQT = 16
H_E = 1408
HS = 2816
NE = 8
NEG = -1.0e30
GDELTA = 0.03
SCALE = float(1.0 / np.sqrt(np.float64(HD)))
EPS = 1e-5
RN = float(np.float32(1.0 / C))
BF = ml_dtypes.bfloat16
ENGINES = ("sync", "tensor", "vector", "scalar", "gpsimd")
DMA_FAMS = ("ci", "cf", "co", "wos", "xs0", "xs1", "l10", "l11", "ckl", "msk",
            "lt0", "lt1", "cq0", "cq1", "ys0", "ys1", "yl0", "yl1", "xq0",
            "xq1", "wo0", "wo1", "wo2", "wo3", "x2s", "l3s0", "l3s1", "b2b", "l3h", "x2r", "out", "lx0", "lx1",
            "lxe", "fs0", "fs1", "lnx", "lxr0", "lxr1", "acs", "aci", "acl",
            "wkv", "wol", "kup", "wq0", "wq1", "gat", "ug0", "ug1",
            "uu0", "uu1", "dnl")


def _rope_tables():
    try:
        import jax
        import jax.numpy as jnp
        with jax.default_device(jax.devices("cpu")[0]):
            freqs = 1.0 / 10000.0 ** (jnp.arange(0, HD, 2, dtype=jnp.float32) / HD)
            emb = jnp.arange(T, dtype=jnp.float32)[:, None] * freqs[None, :]
            cos_h = np.asarray(jnp.cos(emb))
            sin_h = np.asarray(jnp.sin(emb))
    except Exception:
        freqs = (1.0 / (10000.0 ** (np.arange(0, HD, 2, dtype=np.float32) /
                                    np.float32(HD)))).astype(np.float32)
        emb = (np.arange(T, dtype=np.float32)[:, None] * freqs[None, :]
               ).astype(np.float32)
        cos_h = np.cos(emb.astype(np.float64)).astype(np.float32)
        sin_h = np.sin(emb.astype(np.float64)).astype(np.float32)
    cos = np.concatenate([cos_h, cos_h], axis=1)
    sin = np.concatenate([sin_h, sin_h], axis=1)
    return cos, sin


_COS_SIN = _rope_tables()


def _kv_positions(half):
    q0 = half * TLOC
    band = np.clip(np.arange(q0 - 512, q0 + TLOC), 0, T - 1)
    glob = np.arange(NGLOB) * 64
    pad = np.zeros(NGLOB, dtype=np.int64)
    return np.concatenate([band, glob, pad])


def _build_xkv(x_b, half):
    q0 = half * TLOC
    out = np.zeros((NKV, C), dtype=np.float32)
    lo = q0 - 512
    src_lo = max(0, lo)
    out[src_lo - lo: BAND] = x_b[src_lo: q0 + TLOC]
    out[BAND:BAND + NGLOB] = x_b[np.arange(NGLOB) * 64]
    return out


def _build_masks(half):
    q0 = half * TLOC
    m = np.zeros((NQT, 128, NKEY), dtype=np.float32)
    r = np.arange(128)[:, None]
    j = np.arange(640)[None, :]
    univ = (r > j) & (j % 64 != 0)
    for i in range(NQT):
        mi = m[i]
        mi[:, :640][univ] = NEG
        pad = 512 - 128 * i - q0
        if pad > 0:
            mi[:, :pad] = NEG
        g_valid = min(max(0, (q0 + 128 * i - 512) // 64), NGLOB)
        mi[:, 640 + g_valid:] = NEG
    return np.ascontiguousarray(m.astype(BF))


def _fm(a):
    return np.ascontiguousarray(a.T)


def _prep_shared(inputs):
    d = {}
    d["wq_l"] = np.ascontiguousarray(
        np.asarray(inputs["L0_attn_w_q"], np.float32)
        .reshape(KC, 128, NH, 128).transpose(2, 1, 0, 3).reshape(NH, 128, C))
    d["wkvd"] = np.ascontiguousarray(np.asarray(inputs["L0_attn_w_kv_down"], np.float32))
    d["wkup"] = np.ascontiguousarray(np.asarray(inputs["L0_attn_w_k_up"], np.float32))
    d["wvup"] = np.ascontiguousarray(np.asarray(inputs["L0_attn_w_v_up"], np.float32))
    d["wo"] = np.ascontiguousarray(np.asarray(inputs["L0_attn_w_o"], np.float32))
    gT = np.asarray(inputs["L0_ffn_gate_w"], np.float32).T
    d["gate_l"] = np.ascontiguousarray(
        gT.reshape(KC, 128, NE).transpose(1, 0, 2).reshape(128, KC * NE))
    d["bias_bc"] = np.ascontiguousarray(
        np.broadcast_to(np.asarray(inputs["L0_ffn_gate_bias"], np.float32)[None, :],
                        (128, NE)))
    gwT = np.asarray(inputs["L0_ffn_shared_gw"], np.float32).T
    d["gw_l"] = np.ascontiguousarray(
        gwT.reshape(KC, 128, 44, 128).transpose(2, 1, 0, 3).reshape(44, 128, C))
    d["dwT"] = np.ascontiguousarray(np.asarray(inputs["L0_ffn_shared_dw"], np.float32).T)
    upT = np.asarray(inputs["L0_ffn_up_w"], np.float32).transpose(0, 2, 1)
    d["up_l"] = np.ascontiguousarray(
        upT.reshape(NE, KC, 128, 22, 128).transpose(0, 3, 2, 1, 4).reshape(NE, 22, 128, C))
    d["dnT"] = np.ascontiguousarray(
        np.asarray(inputs["L0_ffn_down_w"], np.float32).transpose(0, 2, 1))
    sel = np.zeros((NE, NE * 128), dtype=np.float32)
    for e in range(NE):
        sel[e, e * 128:(e + 1) * 128] = 1.0
    d["esel"] = sel
    d["identb"] = np.eye(128, dtype=np.float32).astype(BF)
    d["identf"] = np.eye(128, dtype=np.float32)
    d["onesf"] = np.ones((1, 128), dtype=np.float32).astype(BF)
    return d


def _prep_core(inputs, shared, b, half):
    cos, sin = _COS_SIN
    q0 = half * TLOC
    x_b = np.asarray(inputs["L0_attn_x"], np.float32)[b]
    d = dict(shared)
    d["xkv"] = _build_xkv(x_b, half)
    kvpos = _kv_positions(half)
    d["cosk"] = _fm(cos[kvpos])
    sk = sin[kvpos]
    d["sink"] = np.ascontiguousarray(
        np.concatenate([-sk[:, :64], sk[:, 64:]], axis=1).T)
    qpos = q0 + np.arange(TLOC)
    d["cosq"] = _fm(cos[qpos])
    sq = sin[qpos]
    d["sinq"] = np.ascontiguousarray(
        np.concatenate([-sq[:, :64], sq[:, 64:]], axis=1).T)
    d["maskq"] = _build_masks(half)
    return d


class _Ph:

    def __init__(self, nc, sems, cnt):
        self.nc = nc
        self.sems = sems
        self.cnt = cnt
        self.q = {e: [] for e in ENGINES}

    def op(self, eng, fn):
        self.q[eng].append(fn)

    def inc(self, eng, fn, sem, by=1):
        s = self.sems[sem]

        def g(e, fn=fn, s=s, by=by):
            fn(e).then_inc(s, by)
        self.q[eng].append(g)
        self.cnt[sem] += by
        return (sem, self.cnt[sem])

    def wait(self, eng, tok):
        if tok is None:
            return
        if isinstance(tok, list):
            best = {}

            def walk(t):
                if t is None:
                    return
                if isinstance(t, list):
                    for t2 in t:
                        walk(t2)
                else:
                    best[t[0]] = max(best.get(t[0], 0), t[1])
            walk(tok)
            for s, v in best.items():
                self.wait(eng, (s, v))
            return
        sem, val = tok
        s = self.sems[sem]

        def g(e, s=s, val=val):
            e.wait_ge(s, val)
        self.q[eng].append(g)

    def drain(self, eng):
        self.q[eng].append(lambda e: e.drain())

    def tok(self, sem):
        return (sem, self.cnt[sem]) if self.cnt[sem] > 0 else None

    def quiesce(self, eng="sync"):
        for name in DMA_FAMS:
            if self.cnt.get(name, 0) > 0:
                self.wait(eng, (name, self.cnt[name]))

    def run(self):
        with self.nc.Block() as blk:
            for eng in ENGINES:
                if self.q[eng]:
                    ops = self.q[eng]

                    def body(e, ops=ops):
                        for f in ops:
                            f(e)
                    getattr(blk, eng)(body)


def _ln_stats(p, st, sumsq_done, veng="vector", seng="scalar"):
    p.wait(veng, sumsq_done)
    p.drain(veng)
    p.op(veng, lambda e: e.tensor_scalar(st[:, 0:1], st[:, 0:1], RN, None, OP.mult))
    p.drain(veng)
    p.op(veng, lambda e: e.tensor_scalar(st[:, 2:3], st[:, 1:2], RN, EPS, OP.mult, OP.add))
    p.drain(veng)
    p.op(veng, lambda e: e.tensor_tensor(st[:, 3:4], st[:, 0:1], st[:, 0:1], OP.mult))
    p.drain(veng)
    vv = p.inc(veng, lambda e: e.tensor_sub(st[:, 2:3], st[:, 2:3], st[:, 3:4]), "vv")
    p.wait(seng, vv)
    p.op(seng, lambda e: e.activation(st[:, 4:5], st[:, 2:3], ACT.Ln))
    p.drain(seng)
    rstd = p.inc(seng, lambda e: e.activation(
        st[:, 5:6], st[:, 4:5], ACT.Exp, scale=-0.5), "ss")
    return rstd


def _build_nc():
    nc = bass.Bass("TRN2", target_bir_lowering=False)
    es = contextlib.ExitStack()
    nc._keepalive = es

    def din(name, shape, dt=F32):
        return nc.dram_tensor(name, shape, dt, kind="ExternalInput").ap()
    xkv = din("xkv", [NKV, C])
    wq_l = din("wq_l", [NH, 128, C])
    wkvd = din("wkvd", [C, LKV])
    wkup = din("wkup", [LKV, LKV])
    wvup = din("wvup", [LKV, LKV])
    wo = din("wo", [C, C])
    gate_l = din("gate_l", [128, KC * NE])
    bias_bc = din("bias_bc", [128, NE])
    gw_l = din("gw_l", [44, 128, C])
    dwT = din("dwT", [HS, C])
    up_l = din("up_l", [NE, 22, 128, C])
    dnT = din("dnT", [NE, H_E, C])
    cosq = din("cosq", [128, TLOC])
    sinq = din("sinq", [128, TLOC])
    cosk = din("cosk", [128, NKV])
    sink = din("sink", [128, NKV])
    maskq = din("maskq", [NQT, 128, NKEY], BF16)
    identb = din("identb", [128, 128], BF16)
    identf = din("identf", [128, 128])
    onesf = din("onesf", [1, 128], BF16)
    esel = din("esel", [NE, NE * 128])
    yout = nc.dram_tensor("yout", [TLOC, C], F32, kind="ExternalOutput").ap()
    wo_sp = nc.dram_tensor("wo_sp", [KC, 128, C], BF16).ap()
    ln1T_sp = nc.dram_tensor("ln1T_sp", [KC, 128, TLOC], BF16).ap()
    y_sp = nc.dram_tensor("y_sp", [NQT, 128, C], BF16).ap()
    x2_sp = nc.dram_tensor("x2_sp", [NQT, 128, C], F32).ap()
    ln3T_sp = nc.dram_tensor("ln3T_sp", [KC, 128, TLOC], BF16).ap()
    lx_sp = nc.dram_tensor("lx_sp", [NQT, 128, 2176], BF16).ap()
    lxe_sp = [nc.dram_tensor(f"lxe_sp{e}", [2048, 2176], BF16).ap()
              for e in range(NE)]
    acc_sp = nc.dram_tensor("acc_sp", [NQT * 128, C], F32).ap()

    sems = {}
    cnt = {}
    for s in ("tt", "vv", "ss", "gpc") + DMA_FAMS:
        sems[s] = es.enter_context(nc.semaphore(s))
        cnt[s] = 0

    idb = es.enter_context(nc.sbuf_tensor("idb", [128, 128], BF16))
    idf = es.enter_context(nc.sbuf_tensor("idf", [128, 128], F32))
    ones_sb = es.enter_context(nc.sbuf_tensor("ones_sb", [1, 128], BF16))
    weT = es.enter_context(nc.sbuf_tensor("weT", [NE, TLOC], F32))
    gweT = es.enter_context(nc.sbuf_tensor("gweT", [NE, TLOC], BF16))

    with contextlib.ExitStack() as psAB:
        Kp = psAB.enter_context(nc.sbuf_tensor("Kp", [128, NKVH * NKV], BF16))
        Vt = psAB.enter_context(nc.sbuf_tensor("Vt", [128, NKVC * 512], BF16))

        with contextlib.ExitStack() as psA:
            def sba(name, shape, dt):
                return psA.enter_context(nc.sbuf_tensor(name, shape, dt))
            ckv_sb = sba("ckv_sb", [128, 4 * NKV], BF16)
            pT = [psA.enter_context(nc.psum_tensor(f"pTa{i}", [128, 1024], BF16))
                  for i in range(2)]
            pA = [psA.enter_context(nc.psum_tensor(f"pAa{i}", [128, 512], F32))
                  for i in range(2)]

            with contextlib.ExitStack() as ps1:
                def sb1(name, shape, dt):
                    return ps1.enter_context(nc.sbuf_tensor(name, shape, dt))
                xs = [sb1(f"xs{i}", [128, C], F32) for i in range(2)]
                sq_scr = sb1("sq_scr", [128, C], F32)
                lnbf = [sb1(f"lnbf{i}", [128, C], BF16) for i in range(2)]
                st_a = sb1("st_a", [128, 8], F32)
                ln1Tb = [sb1(f"ln1Tb{i}", [128, KC * 512], BF16) for i in range(2)]
                wkvd_sb = sb1("wkvd_sb", [128, KC * 512], BF16)
                wostage = sb1("wostage", [128, C], BF16)

                p = _Ph(nc, sems, cnt)
                ld_idb = p.inc("sync", lambda e: e.dma_start(idb[:], identb[:]), "ci", 16)
                p.inc("sync", lambda e: e.dma_start(idf[:], identf[:]), "cf", 16)
                p.inc("sync", lambda e: e.dma_start(ones_sb[:], onesf[:]), "co", 16)
                ld_wkvd = None
                for k in range(KC):
                    ld_wkvd = p.inc("gpsimd", lambda e, k=k: e.dma_start(
                        wkvd_sb[:, k * 512:(k + 1) * 512],
                        wkvd[k * 128:(k + 1) * 128, :]), "wkv", 16)
                last_wo_store = None
                for k in range(KC):
                    p.wait("gpsimd", last_wo_store)
                    tok = p.inc("gpsimd", lambda e, k=k: e.dma_start(
                        wostage[:], wo[k * 128:(k + 1) * 128, :]), "wol", 16)
                    p.wait("sync", tok)
                    last_wo_store = p.inc("sync", lambda e, k=k: e.dma_start(
                        wo_sp[k], wostage[:]), "wos", 16)

                xs_free = [None, None]
                lnbf_free = [None, None]
                ln1Tb_free = [None, None]
                pA_free = [None, None]
                pT_free = [None, None]
                for cc in range(NKVC):
                    par = cc % 2
                    g, c4 = cc // 4, cc % 4
                    gpar = g % 2
                    p.wait("sync", xs_free[par])
                    ld = p.inc("sync", lambda e, cc=cc, par=par: e.dma_start(
                        xs[par][:], xkv[cc * 128:(cc + 1) * 128, :]), f"xs{par}", 16)
                    p.wait("vector", ld)
                    p.drain("vector")
                    p.inc("vector", lambda e, par=par: e.reduce_sum(
                        st_a[:, 0:1], xs[par][:], axis=AX), "vv")
                    p.wait("scalar", ld)
                    sqs = p.inc("scalar", lambda e, par=par: e.activation(
                        sq_scr[:], xs[par][:], ACT.Square, accum_out=st_a[:, 1:2]), "ss")
                    rstd = _ln_stats(p, st_a, [p.tok("vv"), sqs])
                    p.wait("vector", [rstd, lnbf_free[par]])
                    norm = p.inc("vector", lambda e, par=par: e.tensor_scalar(
                        lnbf[par][:], xs[par][:], st_a[:, 0:1], st_a[:, 5:6],
                        OP.subtract, OP.mult), "vv")
                    xs_free[par] = norm
                    p.wait("tensor", [norm, ld_idb, pT_free[0], pT_free[1]])
                    tr = None
                    for k in range(KC):
                        bank, col = k // 8, (k % 8) * 128
                        tr = p.inc("tensor", lambda e, par=par, bank=bank, col=col, k=k:
                                   e.transpose(pT[bank][:, col:col + 128],
                                               lnbf[par][:, k * 128:(k + 1) * 128],
                                               idb[:]), "tt")
                    ev = []
                    for bank in range(2):
                        p.wait("vector", [tr] + ([ln1Tb_free[gpar]] if (c4 == 0 and bank == 0) else []))
                        ev.append(p.inc("vector", lambda e, bank=bank, gpar=gpar, c4=c4:
                                        e.tensor_copy(
                            ln1Tb[gpar].rearrange("p (k c) -> p k c", k=KC)
                            [:, bank * 8:(bank + 1) * 8, c4 * 128:(c4 + 1) * 128],
                            pT[bank].rearrange("p (k c) -> p k c", k=8)[:, :, :]), "vv"))
                    lnbf_free[par] = ev
                    pT_free = [ev[0], ev[1]]
                    last_in_group = (c4 == 3) or (cc == NKVC - 1)
                    if last_in_group:
                        n = (c4 + 1) * 128
                        sp = []
                        if 1 <= g <= 4:
                            for k in range(KC):
                                p.wait("sync", ev)
                                sp.append(p.inc("sync", lambda e, k=k, g=g, gpar=gpar:
                                                e.dma_start(
                                    ln1T_sp[k][:, (g - 1) * 512: g * 512],
                                    ln1Tb[gpar][:, k * 512:(k + 1) * 512]), f"l1{gpar}", 16))
                        mm_done = []
                        for m in range(4):
                            mp = m % 2
                            p.wait("tensor", [pA_free[mp], ev, ld_wkvd])
                            mm = None
                            for k in range(KC):
                                mm = p.inc("tensor", lambda e, m=m, k=k, gpar=gpar, n=n,
                                           mp=mp: e.matmul(
                                    pA[mp][:, 0:n],
                                    wkvd_sb[:, k * 512 + m * 128: k * 512 + (m + 1) * 128],
                                    ln1Tb[gpar][:, k * 512: k * 512 + n],
                                    start=(k == 0), stop=(k == KC - 1)), "tt")
                            p.wait("vector", mm)
                            evk = p.inc("vector", lambda e, m=m, g=g, n=n, mp=mp:
                                        e.tensor_copy(
                                ckv_sb[:, m * NKV + g * 512: m * NKV + g * 512 + n],
                                pA[mp][:, 0:n]), "vv")
                            pA_free[mp] = evk
                            mm_done.append(evk)
                        ln1Tb_free[gpar] = mm_done + sp
                p.quiesce("sync")
                p.run()

            with contextlib.ExitStack() as ps2:
                def sb2(name, shape, dt):
                    return ps2.enter_context(nc.sbuf_tensor(name, shape, dt))
                kup_sb = sb2("kup_sb", [128, 4 * 512], BF16)
                vup_sb = sb2("vup_sb", [128, 4 * 512], BF16)
                cosk_sb = sb2("cosk_sb", [128, NKV], F32)
                sink_sb = sb2("sink_sb", [128, NKV], F32)
                vfm = sb2("vfm", [128, 4 * NKV], BF16)
                rt1 = sb2("rt1", [128, 512], F32)
                rt2 = sb2("rt2", [128, 512], F32)

                p = _Ph(nc, sems, cnt)
                ld_kup = None
                for k in range(4):
                    p.inc("gpsimd", lambda e, k=k: e.dma_start(
                        kup_sb[:, k * 512:(k + 1) * 512],
                        wkup[k * 128:(k + 1) * 128, :]), "kup", 16)
                    ld_kup = p.inc("gpsimd", lambda e, k=k: e.dma_start(
                        vup_sb[:, k * 512:(k + 1) * 512],
                        wvup[k * 128:(k + 1) * 128, :]), "kup", 16)
                ldc = p.inc("sync", lambda e: e.dma_start(cosk_sb[:], cosk[:]), "ckl", 16)
                lds = p.inc("sync", lambda e: e.dma_start(sink_sb[:], sink[:]), "ckl", 16)
                pA_free = [None, None]
                pT_free = [None, None]
                vfm_ready = {}
                for g in range(6):
                    n = 512 if g < 5 else 128
                    for kv in range(NKVH):
                        mp = kv % 2
                        p.wait("tensor", [pA_free[mp], ld_kup])
                        mm = None
                        for k in range(4):
                            mm = p.inc("tensor", lambda e, k=k, kv=kv, g=g, n=n, mp=mp:
                                       e.matmul(
                                pA[mp][:, 0:n],
                                kup_sb[:, k * 512 + kv * 128: k * 512 + (kv + 1) * 128],
                                ckv_sb[:, k * NKV + g * 512: k * NKV + g * 512 + n],
                                start=(k == 0), stop=(k == 3)), "tt")
                        p.wait("vector", [mm, ldc, lds])
                        p.drain("vector")
                        p.op("vector", lambda e, g=g, n=n, mp=mp: e.tensor_mul(
                            rt1[0:64, 0:n], pA[mp][64:128, 0:n],
                            sink_sb[0:64, g * 512:g * 512 + n]))
                        p.op("vector", lambda e, g=g, n=n, mp=mp: e.tensor_mul(
                            rt1[64:128, 0:n], pA[mp][0:64, 0:n],
                            sink_sb[64:128, g * 512:g * 512 + n]))
                        p.op("vector", lambda e, g=g, n=n, mp=mp: e.tensor_mul(
                            rt2[:, 0:n], pA[mp][:, 0:n],
                            cosk_sb[:, g * 512:g * 512 + n]))
                        p.drain("vector")
                        kev = p.inc("vector", lambda e, kv=kv, g=g, n=n: e.tensor_add(
                            Kp[:, kv * NKV + g * 512: kv * NKV + g * 512 + n],
                            rt2[:, 0:n], rt1[:, 0:n]), "vv")
                        pA_free[mp] = kev
                        p.wait("tensor", pA_free[mp])
                        mm = None
                        for k in range(4):
                            mm = p.inc("tensor", lambda e, k=k, kv=kv, g=g, n=n, mp=mp:
                                       e.matmul(
                                pA[mp][:, 0:n],
                                vup_sb[:, k * 512 + kv * 128: k * 512 + (kv + 1) * 128],
                                ckv_sb[:, k * NKV + g * 512: k * NKV + g * 512 + n],
                                start=(k == 0), stop=(k == 3)), "tt")
                        p.wait("vector", mm)
                        vev = p.inc("vector", lambda e, kv=kv, g=g, n=n, mp=mp:
                                    e.tensor_copy(
                            vfm[:, kv * NKV + g * 512: kv * NKV + g * 512 + n],
                            pA[mp][:, 0:n]), "vv")
                        pA_free[mp] = vev
                        vfm_ready[(g, kv)] = vev
                    npc = n // 128
                    for pc2 in range((npc + 1) // 2):
                        bank = pc2 % 2
                        p.wait("tensor", [pT_free[bank]] +
                               [vfm_ready[(g, kv)] for kv in range(NKVH)])
                        tr = None
                        for pc_i in range(2):
                            if pc2 * 2 + pc_i >= npc:
                                continue
                            pc = g * 4 + pc2 * 2 + pc_i
                            for kv in range(NKVH):
                                tr = p.inc("tensor", lambda e, pc=pc, kv=kv, bank=bank,
                                           pc_i=pc_i: e.transpose(
                                    pT[bank][:, pc_i * 512 + kv * 128:
                                             pc_i * 512 + (kv + 1) * 128],
                                    vfm[:, kv * NKV + pc * 128: kv * NKV + (pc + 1) * 128],
                                    idb[:]), "tt")
                        p.wait("vector", tr)
                        nn = min(1024, (npc - pc2 * 2) * 512)
                        vt = p.inc("vector", lambda e, g=g, pc2=pc2, bank=bank, nn=nn:
                                   e.tensor_copy(
                            Vt[:, (g * 4 + pc2 * 2) * 512: (g * 4 + pc2 * 2) * 512 + nn],
                            pT[bank][:, 0:nn]), "vv")
                        pT_free[bank] = vt
                p.quiesce("sync")
                p.run()

        with contextlib.ExitStack() as ps:
            def sb(name, shape, dt):
                return ps.enter_context(nc.sbuf_tensor(name, shape, dt))
            q_sb = sb("q_sb", [128, NH * 512], BF16)
            ln1Tt = [sb(f"ln1Tt{i}", [128, KC * 512], BF16) for i in range(2)]
            masks_sb = sb("masks_sb", [128, NQT * NKEY], BF16)
            cq = [sb(f"cq{i}", [128, 512], F32) for i in range(2)]
            sq_ = [sb(f"sq_{i}", [128, 512], F32) for i in range(2)]
            wqb = [sb(f"wqb{i}", [128, C], BF16) for i in range(2)]
            NSC = 6
            NPR = 8
            sc_sb = [sb(f"sc_sb{i}", [128, NKEY], F32) for i in range(NSC)]
            e_scr = [sb(f"e_scr{i}", [128, NKEY], F32) for i in range(2)]
            pr_bf = [sb(f"pr_bf{i}", [128, NKEY], BF16) for i in range(NPR)]
            prT = sb("prT", [128, 8 * 768], BF16)
            y_sb = [sb(f"y_sb{i}", [128, C], BF16) for i in range(2)]
            st_b = [sb(f"st_b{i}", [128, 4], F32) for i in range(NSC)]
            rt1 = sb("rt1b", [128, 512], F32)
            rt2 = sb("rt2b", [128, 512], F32)
            pS_a = [ps.enter_context(nc.psum_tensor(f"pSa{i}", [128, 512], F32))
                    for i in range(2)]
            pS_b = [ps.enter_context(nc.psum_tensor(f"pSb{i}", [128, 512], F32))
                    for i in range(2)]
            pPT = [ps.enter_context(nc.psum_tensor(f"pPT{i}", [128, 1024], BF16))
                   for i in range(2)]
            pY = [ps.enter_context(nc.psum_tensor(f"pY{i}", [128, 512], F32))
                  for i in range(2)]

            p = _Ph(nc, sems, cnt)
            mld = None
            for i in range(NQT):
                mld = p.inc("sync", lambda e, i=i: e.dma_start(
                    masks_sb[:, i * NKEY:(i + 1) * NKEY], maskq[i]), "msk", 16)
            ln1_free = [None, None]
            cq_free = [None, None]
            wqb_free = [None, None]
            pSa_free = [None, None]
            pSb_free = [None, None]
            pPT_free = [None, None]
            pY_free = [None, None]
            sc_free = [None] * NSC
            pr_free = [None] * NPR
            y_free = [None, None]
            prTset_free = [None, None]
            ctr = {"h": 0, "u": 0, "tr": 0}
            pend = [None]

            def softmax_unit(i, kvg, rope_tok):
                kv = kvg
                heads = []
                for hh in range(4):
                    h = kvg * 4 + hh
                    hp = ctr["h"] % 2
                    ssl = ctr["h"] % NSC
                    psl = ctr["h"] % NPR
                    esl = ctr["h"] % 2
                    ctr["h"] += 1
                    qsl = (h * 512 + (i % 4) * 128, 128)
                    p.wait("tensor", [pSa_free[hp], pSb_free[hp]] +
                           (rope_tok if (hh == 0 and rope_tok) else []))
                    p.inc("tensor", lambda e, qsl=qsl, i=i, kv=kv, hp=hp: e.matmul(
                        pS_a[hp][:], q_sb[:, qsl[0]:qsl[0] + 128],
                        Kp[:, kv * NKV + i * 128: kv * NKV + i * 128 + 512],
                        start=True, stop=True), "tt")
                    p.inc("tensor", lambda e, qsl=qsl, i=i, kv=kv, hp=hp: e.matmul(
                        pS_b[hp][:, 0:128], q_sb[:, qsl[0]:qsl[0] + 128],
                        Kp[:, kv * NKV + i * 128 + 512: kv * NKV + i * 128 + 640],
                        start=True, stop=True), "tt")
                    s3 = p.inc("tensor", lambda e, qsl=qsl, kv=kv, hp=hp: e.matmul(
                        pS_b[hp][:, 128:192], q_sb[:, qsl[0]:qsl[0] + 128],
                        Kp[:, kv * NKV + BAND: kv * NKV + BAND + NGLOB],
                        start=True, stop=True), "tt")
                    p.wait("vector", [s3, mld, sc_free[ssl]])
                    p.drain("vector")
                    p.op("vector", lambda e, i=i, hp=hp, ssl=ssl: e.scalar_tensor_tensor(
                        sc_sb[ssl][:, 0:512], pS_a[hp][:], SCALE,
                        masks_sb[:, i * NKEY: i * NKEY + 512], OP.mult, OP.add))
                    ev1 = p.inc("vector", lambda e, i=i, hp=hp, ssl=ssl:
                                e.scalar_tensor_tensor(
                        sc_sb[ssl][:, 512:704], pS_b[hp][:, 0:192], SCALE,
                        masks_sb[:, i * NKEY + 512: i * NKEY + 704],
                        OP.mult, OP.add), "vv")
                    pSa_free[hp] = ev1
                    pSb_free[hp] = ev1
                    p.drain("vector")
                    mx = p.inc("vector", lambda e, ssl=ssl: e.reduce_max(
                        st_b[ssl][:, 0:1], sc_sb[ssl][:], axis=AX, negate=True), "vv")
                    p.wait("scalar", mx)
                    p.inc("scalar", lambda e, ssl=ssl, esl=esl: e.activation(
                        e_scr[esl][:], sc_sb[ssl][:], ACT.Exp, bias=st_b[ssl][:, 0:1],
                        accum_out=st_b[ssl][:, 1:2]), "ss")
                    p.drain("scalar")
                    p.op("scalar", lambda e, ssl=ssl: e.activation(
                        st_b[ssl][:, 2:3], st_b[ssl][:, 1:2], ACT.Ln))
                    p.drain("scalar")
                    p.op("scalar", lambda e, ssl=ssl: e.activation(
                        st_b[ssl][:, 3:4], st_b[ssl][:, 2:3], ACT.Identity,
                        scale=-1.0, bias=st_b[ssl][:, 0:1]))
                    p.drain("scalar")
                    p.wait("scalar", pr_free[psl])
                    e3 = p.inc("scalar", lambda e, ssl=ssl, psl=psl: e.activation(
                        pr_bf[psl][:], sc_sb[ssl][:], ACT.Exp,
                        bias=st_b[ssl][:, 3:4]), "ss")
                    sc_free[ssl] = e3
                    heads.append((psl, e3))
                return (i, kvg, heads)

            def flush_unit(u):
                i, kvg, heads = u
                kv = kvg
                uset = ctr["u"] % 2
                ctr["u"] += 1
                for hh, (psl, e3) in enumerate(heads):
                    pp = ctr["tr"] % 2
                    ctr["tr"] += 1
                    p.wait("tensor", [e3, pPT_free[pp]])
                    for c2 in range(5):
                        p.inc("tensor", lambda e, c2=c2, psl=psl, pp=pp: e.transpose(
                            pPT[pp][:, c2 * 128:(c2 + 1) * 128],
                            pr_bf[psl][:, c2 * 128:(c2 + 1) * 128], idb[:]), "tt")
                    tr = p.inc("tensor", lambda e, psl=psl, pp=pp: e.transpose(
                        pPT[pp][0:64, 640:768],
                        pr_bf[psl][:, 640:704], idb[:]), "tt")
                    pr_free[psl] = tr
                    p.wait("scalar", [tr, prTset_free[uset]])
                    p.inc("scalar", lambda e, hh=hh, pp=pp, uset=uset: e.copy(
                        prT[:, uset * 3072 + hh * 768: uset * 3072 + hh * 768 + 640],
                        pPT[pp][:, 0:640]), "ss")
                    p.drain("scalar")
                    pe2 = p.inc("scalar", lambda e, hh=hh, pp=pp, uset=uset: e.copy(
                        prT[0:64, uset * 3072 + hh * 768 + 640:
                            uset * 3072 + hh * 768 + 768],
                        pPT[pp][0:64, 640:768]), "ss")
                    pPT_free[pp] = pe2
                    heads[hh] = (psl, e3, pe2)
                yp = ctr["u"] % 2
                p.wait("tensor", [h[2] for h in heads] + [pY_free[yp]])
                mm = None
                first = True
                for c2 in range(6):
                    if c2 < 5:
                        vof, kof, kdim = (i + c2) * 512 + kv * 128, c2 * 128, 128
                    else:
                        vof, kof, kdim = 20 * 512 + kv * 128, 640, 64
                    for hh in range(4):
                        mm = p.inc("tensor",
                                   lambda e, vof=vof, kof=kof, kdim=kdim, hh=hh,
                                   yp=yp, first=first, c2=c2, uset=uset: e.matmul(
                            pY[yp][:, hh * 128:(hh + 1) * 128],
                            Vt[0:kdim, vof:vof + 128],
                            prT[0:kdim, uset * 3072 + hh * 768 + kof:
                                uset * 3072 + hh * 768 + kof + 128],
                            start=first, stop=(c2 == 5 and hh == 3),
                            skip_group_check=True), "tt")
                        first = False
                prTset_free[uset] = mm
                yqp = (i % 4) % 2
                p.wait("vector", [mm] + ([y_free[yqp]] if kvg == 0 else []))
                yev = p.inc("vector", lambda e, kvg=kvg, yp=yp, yqp=yqp: e.tensor_copy(
                    y_sb[yqp][:, kvg * 512:(kvg + 1) * 512], pY[yp][:]), "vv")
                pY_free[yp] = yev
                if kvg == 3:
                    p.wait("sync", yev)
                    y_free[yqp] = p.inc("sync", lambda e, i=i, yqp=yqp: e.dma_start(
                        y_sp[i], y_sb[yqp][:]), f"ys{yqp}", 16)

            for tt in range(4):
                tp = tt % 2
                p.wait("sync", ln1_free[tp])
                lnld = None
                for k in range(KC):
                    lnld = p.inc("sync", lambda e, k=k, tt=tt, tp=tp: e.dma_start(
                        ln1Tt[tp][:, k * 512:(k + 1) * 512],
                        ln1T_sp[k][:, tt * 512:(tt + 1) * 512]), f"lt{tp}", 16)
                p.wait("sync", cq_free[tp])
                cld = p.inc("sync", lambda e, tt=tt, tp=tp: e.dma_start(
                    cq[tp][:], cosq[:, tt * 512:(tt + 1) * 512]), f"cq{tp}", 16)
                cld2 = p.inc("sync", lambda e, tt=tt, tp=tp: e.dma_start(
                    sq_[tp][:], sinq[:, tt * 512:(tt + 1) * 512]), f"cq{tp}", 16)
                rope_done = []
                for m in range(NH):
                    mp = m % 2
                    p.wait("gpsimd", wqb_free[mp])
                    wld = p.inc("gpsimd", lambda e, m=m, mp=mp: e.dma_start(
                        wqb[mp][:], wq_l[m]), f"wq{mp}", 16)
                    p.wait("tensor", [wld, lnld, pSa_free[mp]])
                    mm = None
                    for k in range(KC):
                        mm = p.inc("tensor", lambda e, m=m, k=k, tp=tp, mp=mp: e.matmul(
                            pS_a[mp][:],
                            wqb[mp][:, k * 128:(k + 1) * 128],
                            ln1Tt[tp][:, k * 512:(k + 1) * 512],
                            start=(k == 0), stop=(k == KC - 1)), "tt")
                    wqb_free[mp] = mm
                    p.wait("vector", [mm, cld, cld2])
                    p.drain("vector")
                    p.op("vector", lambda e, tp=tp, mp=mp: e.tensor_mul(
                        rt1[0:64, :], pS_a[mp][64:128, :], sq_[tp][0:64, :]))
                    p.op("vector", lambda e, tp=tp, mp=mp: e.tensor_mul(
                        rt1[64:128, :], pS_a[mp][0:64, :], sq_[tp][64:128, :]))
                    p.op("vector", lambda e, tp=tp, mp=mp: e.tensor_mul(
                        rt2[:, :], pS_a[mp][:, :], cq[tp][:, :]))
                    p.drain("vector")
                    rp = p.inc("vector", lambda e, m=m: e.tensor_add(
                        q_sb[:, m * 512:(m + 1) * 512], rt2[:], rt1[:]), "vv")
                    pSa_free[mp] = rp
                    rope_done.append(rp)
                for qt in range(4):
                    i = tt * 4 + qt
                    for kvg in range(4):
                        u = softmax_unit(i, kvg,
                                         rope_done if (qt == 0 and kvg == 0) else None)
                        if pend[0] is not None:
                            flush_unit(pend[0])
                        pend[0] = u
                ln1_free[tp] = p.tok("tt")
                cq_free[tp] = p.tok("vv")
            flush_unit(pend[0])
            pend[0] = None
            p.quiesce("sync")
            p.run()

    with contextlib.ExitStack() as ps:
        def sb(name, shape, dt):
            return ps.enter_context(nc.sbuf_tensor(name, shape, dt))
        ysb = [sb(f"ysb{i}", [128, C], BF16) for i in range(2)]
        xq = [sb(f"xq{i}", [128, C], F32) for i in range(2)]
        wob = [sb(f"wob{i}", [128, C], BF16) for i in range(4)]
        att_sb = sb("att_sb", [128, C], F32)
        sq_scr = sb("sq_scrb", [128, C], F32)
        x2_sb = sb("x2_sb", [128, C], F32)
        ln3_bf = [sb(f"ln3_bf{i}", [128, C], BF16) for i in range(2)]
        ln3Tt = [sb(f"ln3Tt{i}", [128, KC * 128], BF16) for i in range(2)]
        gate_sb = sb("gate_sb", [128, KC * NE], BF16)
        bias_sb = sb("bias_sb", [128, NE], F32)
        s_sb = sb("s_sb", [128, NE], F32)
        sbb = sb("sbb", [128, NE], F32)
        eqm = sb("eqm", [128, NE], F32)
        wsel = sb("wsel", [128, NE], F32)
        st_c = sb("st_c", [128, 8], F32)
        st_w = sb("st_w", [128, 4], F32)
        gse = sb("gse", [128, NE], F32)
        we_ext = sb("we_ext", [128, 128], BF16)
        iota_p = sb("iota_p", [128, 1], mybir.dt.int32)
        pW = ps.enter_context(nc.psum_tensor("pW", [128, 2048], F32))
        pTc = ps.enter_context(nc.psum_tensor("pTc", [128, 1024], BF16))
        pG = ps.enter_context(nc.psum_tensor("pG", [128, NE], F32))
        pX = ps.enter_context(nc.psum_tensor("pX", [128, 512], F32))

        p = _Ph(nc, sems, cnt)
        gld = p.inc("gpsimd", lambda e: e.dma_start(gate_sb[:], gate_l[:]), "gat", 16)
        bld = p.inc("sync", lambda e: e.dma_start(bias_sb[:], bias_bc[:]), "b2b", 16)
        iog = p.inc("gpsimd", lambda e: e.iota(iota_p[:], pattern=[[0, 1]], base=0,
                                               channel_multiplier=1), "gpc")
        p.wait("vector", iog)
        p.op("vector", lambda e: e.tensor_copy(we_ext[:, 9:10], iota_p[:]))
        p.drain("vector")
        p.op("vector", lambda e: e.memset(we_ext[:, 0:9], 0.0))
        p.drain("vector")
        wem = p.inc("vector", lambda e: e.memset(we_ext[:, 10:128], 0.0), "vv")
        p.drain("vector")
        y_free = [None, None]
        xq_free = [None, None]
        wob_free = [None] * 4
        st = {"pW": None, "pTc": None, "pG": None, "pX": None, "att": None,
              "x2": None, "ln3bf": [None, None], "ln3Tt": [None, None],
              "lxd": [None, None], "wex": None}
        stage2_in = [None] * NQT

        def stage1(i):
            ip = i % 2
            p.wait("scalar", y_free[ip])
            yld = p.inc("scalar", lambda e, i=i, ip=ip: e.dma_start(
                ysb[ip][:], y_sp[i]), f"yl{ip}", 16)
            p.wait("scalar", xq_free[ip])
            xld = p.inc("scalar", lambda e, i=i, ip=ip: e.dma_start(
                xq[ip][:], xkv[512 + i * 128: 512 + (i + 1) * 128, :]), f"xq{ip}", 16)
            p.wait("tensor", st["pW"])
            mm = None
            for k in range(KC):
                kp = k % 4
                p.wait("gpsimd", wob_free[kp])
                wld = p.inc("gpsimd", lambda e, k=k, kp=kp: e.dma_start(
                    wob[kp][:], wo_sp[k]), f"wo{kp}", 16)
                p.wait("tensor", [wld] + ([yld] if k == 0 else []))
                for n in range(4):
                    mm = p.inc("tensor", lambda e, k=k, n=n, ip=ip, kp=kp: e.matmul(
                        pW[:, n * 512:(n + 1) * 512],
                        ysb[ip][:, k * 128:(k + 1) * 128],
                        wob[kp][:, n * 512:(n + 1) * 512],
                        start=(k == 0), stop=(k == KC - 1)), "tt")
                wob_free[kp] = mm
            y_free[ip] = mm
            p.wait("scalar", [mm, st["att"], st.get("stc")])
            p.inc("scalar", lambda e: e.activation(
                att_sb[:], pW[:], ACT.Copy, accum_out=st_c[:, 0:1]), "ss")
            p.drain("scalar")
            sq1 = p.inc("scalar", lambda e: e.activation(
                sq_scr[:], pW[:], ACT.Square, accum_out=st_c[:, 1:2]), "ss")
            st["pW"] = sq1
            rstd = _ln_stats(p, st_c, sq1)
            p.wait("vector", [rstd, xld, st["x2"]])
            p.drain("vector")
            p.op("vector", lambda e: e.tensor_scalar(
                sq_scr[:], att_sb[:], st_c[:, 0:1], st_c[:, 5:6], OP.subtract, OP.mult))
            p.drain("vector")
            x2d = p.inc("vector", lambda e, ip=ip: e.tensor_add(
                x2_sb[:], xq[ip][:], sq_scr[:]), "vv")
            st["att"] = x2d
            xq_free[ip] = x2d
            p.wait("sync", x2d)
            x2st = p.inc("sync", lambda e, i=i: e.dma_start(x2_sp[i], x2_sb[:]), "x2s", 16)
            p.drain("vector")
            p.inc("vector", lambda e: e.reduce_sum(st_c[:, 0:1], x2_sb[:], axis=AX), "vv")
            p.wait("scalar", x2d)
            sq3 = p.inc("scalar", lambda e: e.activation(
                sq_scr[:], x2_sb[:], ACT.Square, accum_out=st_c[:, 1:2]), "ss")
            rstd3 = _ln_stats(p, st_c, [p.tok("vv"), sq3])
            p.wait("vector", [rstd3, st["ln3bf"][ip]])
            ln3d = p.inc("vector", lambda e, ip=ip: e.tensor_scalar(
                ln3_bf[ip][:], x2_sb[:], st_c[:, 0:1], st_c[:, 5:6],
                OP.subtract, OP.mult), "vv")
            st["x2"] = [ln3d, x2st]
            st["stc"] = ln3d
            p.wait("sync", ln3d)
            lxd = p.inc("sync", lambda e, i=i, ip=ip: e.dma_start(
                lx_sp[i][:, 0:2048], ln3_bf[ip][:]), f"lx{ip}", 16)
            st["lxd"][ip] = lxd
            stage2_in[i] = ln3d

        def stage2(i):
            ip = i % 2
            ln3d = stage2_in[i]
            p.wait("tensor", [ln3d, st["pTc"]])
            evs = []
            for half_k in range(2):
                tr = None
                for k8 in range(8):
                    k = half_k * 8 + k8
                    tr = p.inc("tensor", lambda e, k=k, k8=k8, ip=ip: e.transpose(
                        pTc[:, k8 * 128:(k8 + 1) * 128],
                        ln3_bf[ip][:, k * 128:(k + 1) * 128], idb[:]), "tt")
                p.wait("vector", [tr, st["ln3Tt"][ip]])
                evs.append(p.inc("vector", lambda e, half_k=half_k, ip=ip: e.tensor_copy(
                    ln3Tt[ip][:, half_k * 1024:(half_k + 1) * 1024], pTc[:]), "vv"))
                if half_k == 0:
                    p.wait("tensor", evs[-1])
            st["pTc"] = evs[-1]
            st["ln3bf"][ip] = [evs[-1], st["lxd"][ip]]
            sp = []
            for k in range(KC):
                p.wait("sync", evs[k // 8])
                sp.append(p.inc("sync", lambda e, k=k, i=i, ip=ip: e.dma_start(
                    ln3T_sp[k][:, i * 128:(i + 1) * 128],
                    ln3Tt[ip][:, k * 128:(k + 1) * 128]), f"l3s{ip}", 16))
            p.wait("tensor", [evs[-1], gld, st["pG"]])
            gm = None
            for k in range(KC):
                gm = p.inc("tensor", lambda e, k=k, ip=ip: e.matmul(
                    pG[:], ln3Tt[ip][:, k * 128:(k + 1) * 128],
                    gate_sb[:, k * NE:(k + 1) * NE],
                    start=(k == 0), stop=(k == KC - 1)), "tt")
            st["ln3Tt"][ip] = [gm] + sp
            p.wait("scalar", gm)
            sg = p.inc("scalar", lambda e: e.activation(s_sb[:], pG[:], ACT.Sigmoid), "ss")
            st["pG"] = sg
            p.wait("vector", [sg, bld])
            p.drain("vector")
            p.op("vector", lambda e: e.tensor_add(sbb[:], s_sb[:], bias_sb[:]))
            p.drain("vector")
            p.op("vector", lambda e: e.reduce_max(st_w[:, 0:1], sbb[:], axis=AX))
            p.drain("vector")
            p.op("vector", lambda e: e.tensor_scalar(
                eqm[:], sbb[:], st_w[:, 0:1], None, OP.is_equal))
            p.drain("vector")
            p.op("vector", lambda e: e.scalar_tensor_tensor(
                eqm[:], eqm[:], NEG, sbb[:], OP.mult, OP.add))
            p.drain("vector")
            p.op("vector", lambda e: e.reduce_max(st_w[:, 0:1], eqm[:], axis=AX))
            p.drain("vector")
            p.op("vector", lambda e: e.tensor_scalar_add(st_w[:, 2:3], st_w[:, 0:1], -GDELTA))
            p.drain("vector")
            p.op("vector", lambda e: e.tensor_scalar(
                gse[:], sbb[:], st_w[:, 2:3], None, OP.is_ge))
            p.drain("vector")
            p.op("vector", lambda e: e.tensor_scalar(
                eqm[:], sbb[:], st_w[:, 0:1], None, OP.is_ge))
            p.drain("vector")
            p.op("vector", lambda e: e.tensor_mul(wsel[:], s_sb[:], eqm[:]))
            p.drain("vector")
            p.op("vector", lambda e: e.reduce_sum(st_w[:, 0:1], wsel[:], axis=AX))
            p.drain("vector")
            p.op("vector", lambda e: e.reciprocal(st_w[:, 1:2], st_w[:, 0:1]))
            p.drain("vector")
            wed = p.inc("vector", lambda e: e.tensor_scalar(
                wsel[:], wsel[:], st_w[:, 1:2], None, OP.mult), "vv")
            p.wait("vector", [st["wex"], wem])
            p.drain("vector")
            p.op("vector", lambda e: e.tensor_copy(we_ext[:, 0:8], wsel[:]))
            p.drain("vector")
            wxd = p.inc("vector", lambda e, i=i: e.memset(we_ext[:, 8:9], float(i)), "vv")
            p.wait("sync", wxd)
            st["wex"] = p.inc("sync", lambda e, i=i: e.dma_start(
                lx_sp[i][:, 2048:2176], we_ext[:]), "lxe", 16)
            p.wait("tensor", [wed, st["pX"]])
            p.inc("tensor", lambda e: e.transpose(
                pX[0:NE, 0:128], wsel[:], idf[:]), "tt")
            wt = p.inc("tensor", lambda e: e.transpose(
                pX[0:NE, 128:256], gse[:], idf[:]), "tt")
            p.wait("vector", wt)
            p.op("vector", lambda e, i=i: e.tensor_copy(
                weT[:, i * 128:(i + 1) * 128], pX[0:NE, 0:128]))
            st["pX"] = p.inc("vector", lambda e, i=i: e.tensor_copy(
                gweT[:, i * 128:(i + 1) * 128], pX[0:NE, 128:256]), "vv")

        for i in range(NQT + 1):
            if i < NQT:
                stage1(i)
            if i >= 1:
                stage2(i - 1)
        p.quiesce("sync")
        p.run()

    ECAP = 1024
    with contextlib.ExitStack() as ps:
        def sb(name, shape, dt):
            return ps.enter_context(nc.sbuf_tensor(name, shape, dt))
        ln3h = sb("ln3h", [128, KC * 1024], BF16)
        h_sb = sb("h_sb", [128, 11 * 1024], BF16)
        dn_sb = sb("dn_sb", [128, 11 * 2048], BF16)
        upg = [sb(f"upg{i}", [128, C], BF16) for i in range(2)]
        upu = [sb(f"upu{i}", [128, C], BF16) for i in range(2)]
        lnx = sb("lnx", [128, 4 * 2176], BF16)
        lnx_fm = sb("lnx_fm", [128, KC * 1024], BF16)
        stage = [sb(f"stage{i}", [128, C], F32) for i in range(2)]
        cumC = sb("cumC", [8, TLOC], F32)[:]
        selA = stage[0][0:8, :]
        onesB = stage[1][0:8, :]
        slots_i = sb("slots_i", [128, 128], mybir.dt.int32)
        wecol = sb("wecol", [128, 8], F32)
        tokf = sb("tokf", [128, 8], F32)
        toki = sb("toki", [128, 8], mybir.dt.int32)
        sg_bf = sb("sg_bf", [128, 512], BF16)
        uw_bf = sb("uw_bf", [128, 512], BF16)
        st_d = sb("st_d", [128, 8], F32)
        pGg = ps.enter_context(nc.psum_tensor("pGg", [128, 512], F32))
        pGu = ps.enter_context(nc.psum_tensor("pGu", [128, 512], F32))
        pD = ps.enter_context(nc.psum_tensor("pD", [128, 2048], F32))
        pTe = ps.enter_context(nc.psum_tensor("pTe", [128, 1024], BF16))
        pX2 = ps.enter_context(nc.psum_tensor("pX2", [128, 512], F32))

        upg_free = [None, None]
        upu_free = [None, None]
        dn_free = None
        h_free = [None] * 11
        pGg_free = None
        pGu_free = None
        pD_free = None
        pTe_free = None
        sg_free = None
        uw_free = None
        acc_chain = None
        lnx_free = None
        scat_done = None
        sli = None

        def emit_branch(p, kind, be, hf, lnx_ready, dn_srcs):
            nonlocal dn_free, pGg_free, pGu_free, pD_free, sg_free, uw_free
            nonlocal acc_chain
            ngrp = 2
            for j in range(11):
                jp = j % 2
                if kind == "sha":
                    mg, mu = 22 + j, j
                elif kind == "shb":
                    mg, mu = 33 + j, 11 + j
                else:
                    mg, mu = j, 11 + j
                p.wait("gpsimd", upg_free[jp])
                if kind == "e":
                    gl = p.inc("gpsimd", lambda e, be=be, mg=mg, jp=jp: e.dma_start(
                        upg[jp][:], up_l[be, mg]), f"ug{jp}", 16)
                else:
                    gl = p.inc("gpsimd", lambda e, mg=mg, jp=jp: e.dma_start(
                        upg[jp][:], gw_l[mg]), f"ug{jp}", 16)
                p.wait("gpsimd", upu_free[jp])
                if kind == "e":
                    ul = p.inc("gpsimd", lambda e, be=be, mu=mu, jp=jp: e.dma_start(
                        upu[jp][:], up_l[be, mu]), f"uu{jp}", 16)
                else:
                    ul = p.inc("gpsimd", lambda e, mu=mu, jp=jp: e.dma_start(
                        upu[jp][:], gw_l[mu]), f"uu{jp}", 16)
                for n2 in range(ngrp):
                    def rhs_ap(k, n2=n2, kind=kind):
                        if kind == "e":
                            return lnx_fm[:, k * 1024 + n2 * 512: k * 1024 + (n2 + 1) * 512]
                        return ln3h[:, k * 1024 + n2 * 512: k * 1024 + (n2 + 1) * 512]
                    p.wait("tensor", [gl, pGg_free] + ([lnx_ready] if n2 == 0 else []))
                    mmg = None
                    for k in range(KC):
                        mmg = p.inc("tensor", lambda e, k=k, jp=jp, rhs_ap=rhs_ap: e.matmul(
                            pGg[:], upg[jp][:, k * 128:(k + 1) * 128], rhs_ap(k),
                            start=(k == 0), stop=(k == KC - 1)), "tt")
                    p.wait("tensor", [ul, pGu_free])
                    mmu = None
                    for k in range(KC):
                        mmu = p.inc("tensor", lambda e, k=k, jp=jp, rhs_ap=rhs_ap: e.matmul(
                            pGu[:], upu[jp][:, k * 128:(k + 1) * 128], rhs_ap(k),
                            start=(k == 0), stop=(k == KC - 1)), "tt")
                    if n2 == ngrp - 1:
                        upg_free[jp] = mmg
                        upu_free[jp] = mmu
                    p.wait("scalar", [mmg, sg_free])
                    sgd = p.inc("scalar", lambda e: e.activation(
                        sg_bf[:], pGg[:], ACT.Silu), "ss")
                    pGg_free = sgd
                    p.wait("vector", [mmu, uw_free])
                    p.drain("vector")
                    uwd = p.inc("vector", lambda e: e.tensor_copy(uw_bf[:], pGu[:]), "vv")
                    pGu_free = uwd
                    p.wait("vector", [sgd, h_free[j]])
                    p.drain("vector")
                    hd = p.inc("vector", lambda e, j=j, n2=n2: e.tensor_mul(
                        h_sb[:, j * 1024 + n2 * 512: j * 1024 + (n2 + 1) * 512],
                        sg_bf[:], uw_bf[:]), "vv")
                    sg_free = hd
                    uw_free = hd
                    h_free[j] = hd
                p.wait("gpsimd", dn_free)
                dl = p.inc("gpsimd", lambda e, j=j, src=dn_srcs[j]: e.dma_start(
                    dn_sb[:, j * 2048:(j + 1) * 2048], src), "dnl", 16)
                if j == 10:
                    p._dn_tok = dl
            ntile = 8
            for qt in range(ntile):
                sp_ = qt % 2
                p.wait("tensor", [pD_free] +
                       (([h_free[jj] for jj in range(11)] + [p._dn_tok]) if qt == 0 else []))
                mm = None
                for j in range(11):
                    for n in range(4):
                        mm = p.inc("tensor", lambda e, j=j, n=n, qt=qt: e.matmul(
                            pD[:, n * 512:(n + 1) * 512],
                            h_sb[:, j * 1024 + qt * 128: j * 1024 + qt * 128 + 128],
                            dn_sb[:, j * 2048 + n * 512: j * 2048 + (n + 1) * 512],
                            start=(j == 0), stop=(j == 10)), "tt")
                p.wait("vector", [mm, acc_chain] + ([None] if kind != "e" else []))
                p.drain("vector")
                if kind == "e":
                    ev = p.inc("vector", lambda e, sp_=sp_, qt=qt: e.tensor_scalar(
                        stage[sp_][:], pD[:], wecol[:, qt:qt + 1], None, OP.mult), "vv")
                else:
                    ev = p.inc("vector", lambda e, sp_=sp_: e.tensor_copy(
                        stage[sp_][:], pD[:]), "vv")
                pD_free = ev
                if kind == "sha":
                    p.wait("sync", [ev, acc_chain])
                    acc_chain = p.inc("sync", lambda e, hf=hf, qt=qt, sp_=sp_: e.dma_start(
                        acc_sp[(hf * 8 + qt) * 128:(hf * 8 + qt + 1) * 128, :],
                        stage[sp_][:]), "aci", 16)
                elif kind == "shb":
                    p.wait("gpsimd", [ev, acc_chain])
                    acc_chain = p.inc("gpsimd", lambda e, hf=hf, qt=qt, sp_=sp_: e.dma_start(
                        acc_sp[(hf * 8 + qt) * 128:(hf * 8 + qt + 1) * 128, :],
                        stage[sp_][:], accum_op=OP.add), "acs", 16)
                else:
                    p.wait("gpsimd", [ev, acc_chain, p._tokd])
                    acc_chain = p.inc("gpsimd", lambda e, qt=qt, sp_=sp_: e.indirect_dma_start(
                        acc_sp[:],
                        bass.IndirectOffsetOnAxis(toki[:, qt:qt + 1], 0),
                        stage[sp_][:], None, compute_op=OP.add), "acs", 16)
            mm_last = p.tok("tt")
            dn_free = mm_last
            for jj in range(11):
                h_free[jj] = mm_last

        p = _Ph(nc, sems, cnt)
        prev_tt = p.tok("tt")
        lnld = None
        for k in range(KC):
            if k == 0:
                p.wait("sync", prev_tt)
            lnld = p.inc("sync", lambda e, k=k: e.dma_start(
                ln3h[:, k * 1024:(k + 1) * 1024], ln3T_sp[k][:, 0:1024]), "l3h", 16)
        p.op("vector", lambda e: e.tensor_single_scalar(
            selA, gweT[:], 0.5, OP.is_gt))
        p.drain("vector")
        p.op("vector", lambda e: e.memset(onesB, 1.0))
        p.drain("vector")
        vscan = p.inc("vector", lambda e: e.tensor_tensor_scan(
            cumC, onesB, selA, 0.0, OP.mult, OP.add), "vv")
        p.wait("gpsimd", vscan)
        iog = p.inc("gpsimd", lambda e: e.iota(
            onesB, pattern=[[1, TLOC]], base=0, channel_multiplier=0,
            allow_small_or_imprecise_dtypes=True), "gpc")
        p.wait("vector", iog)
        p.drain("vector")
        p.op("vector", lambda e: e.tensor_sub(onesB, onesB, cumC))
        p.drain("vector")
        p.op("vector", lambda e: e.tensor_scalar_add(
            onesB, onesB, cumC[:, TLOC - 1:TLOC]))
        p.drain("vector")
        p.op("vector", lambda e: e.tensor_scalar_add(cumC, cumC, -1.0))
        p.drain("vector")
        p.op("vector", lambda e: e.tensor_sub(cumC, cumC, onesB))
        p.drain("vector")
        p.op("vector", lambda e: e.tensor_mul(selA, selA, cumC))
        p.drain("vector")
        sld = p.inc("vector", lambda e: e.tensor_add(onesB, onesB, selA), "vv")
        cp_tok = None
        for rnd in range(2):
            p.wait("tensor", [sld, cp_tok])
            trs = None
            for c8 in range(8):
                c = rnd * 8 + c8
                trs = p.inc("tensor", lambda e, c=c, c8=c8: e.transpose(
                    pX2[:, c8 * 8: c8 * 8 + 8],
                    onesB[:, c * 128:(c + 1) * 128], idf[0:8, 0:8]), "tt")
            p.wait("vector", trs)
            p.drain("vector")
            cp_tok = p.inc("vector", lambda e, rnd=rnd: e.tensor_copy(
                slots_i[:, rnd * 64: rnd * 64 + 64], pX2[:, 0:64]), "vv")
        sli = cp_tok
        scat_tok = [None, None]
        for cch in range(KC):
            cp = cch % 2
            p.wait("scalar", scat_tok[cp])
            lr = p.inc("scalar", lambda e, cch=cch, cp=cp: e.dma_start(
                lnx[:, cp * 2176:(cp + 1) * 2176], lx_sp[cch]), f"lxr{cp}", 16)
            p.wait("gpsimd", [lr, sli])
            last = None
            for e2 in range(NE):
                last = p.inc("gpsimd", lambda e, e2=e2, cch=cch, cp=cp:
                             e.indirect_dma_start(
                    lxe_sp[e2],
                    bass.IndirectOffsetOnAxis(
                        slots_i[:, cch * 8 + e2: cch * 8 + e2 + 1], 0),
                    lnx[:, cp * 2176:(cp + 1) * 2176], None), f"fs{cp}", 16)
            scat_tok[cp] = last
        scat_done = [scat_tok[0], scat_tok[1]]
        dn_sha = [dwT[j * 128:(j + 1) * 128, :] for j in range(11)]
        dn_shb = [dwT[(11 + j) * 128:(12 + j) * 128, :] for j in range(11)]
        emit_branch(p, "sha", 0, 0, lnld, dn_sha)
        emit_branch(p, "shb", 0, 0, lnld, dn_shb)
        p.quiesce("sync")
        p.run()

        p = _Ph(nc, sems, cnt)
        lnld = None
        for k in range(KC):
            lnld = p.inc("sync", lambda e, k=k: e.dma_start(
                ln3h[:, k * 1024:(k + 1) * 1024], ln3T_sp[k][:, 1024:2048]), "l3h", 16)
        emit_branch(p, "sha", 0, 1, lnld, dn_sha)
        emit_branch(p, "shb", 0, 1, lnld, dn_shb)
        p.quiesce("sync")
        p.run()

        p = _Ph(nc, sems, cnt)
        for be in range(NE):
            dn_e = [dnT[be, j * 128:(j + 1) * 128, :] for j in range(11)]
            tokd = None
            lnx_ready = None
            for sg2 in range(2):
                p.wait("scalar", [scat_done, lnx_free])
                lxl = None
                for cch in range(4):
                    lxl = p.inc("scalar", lambda e, be=be, sg2=sg2, cch=cch: e.dma_start(
                        lnx[:, cch * 2176:(cch + 1) * 2176],
                        lxe_sp[be][sg2 * 512 + cch * 128: sg2 * 512 + (cch + 1) * 128, :]),
                        "lnx", 16)
                p.wait("vector", [lxl, acc_chain, tokd])
                p.drain("vector")
                for cch in range(4):
                    qt8 = sg2 * 4 + cch
                    p.op("vector", lambda e, cch=cch, be=be, qt8=qt8: e.tensor_copy(
                        wecol[:, qt8:qt8 + 1],
                        lnx[:, cch * 2176 + 2048 + be: cch * 2176 + 2048 + be + 1]))
                    p.op("vector", lambda e, cch=cch, qt8=qt8: e.scalar_tensor_tensor(
                        tokf[:, qt8:qt8 + 1],
                        lnx[:, cch * 2176 + 2048 + 8: cch * 2176 + 2048 + 9],
                        128.0,
                        lnx[:, cch * 2176 + 2048 + 9: cch * 2176 + 2048 + 10],
                        OP.mult, OP.add))
                p.drain("vector")
                tokd = p.inc("vector", lambda e, sg2=sg2: e.tensor_copy(
                    toki[:, sg2 * 4:(sg2 + 1) * 4], tokf[:, sg2 * 4:(sg2 + 1) * 4]), "vv")
                tcnt = 0
                evs = []
                for cch in range(4):
                    for k in range(KC):
                        if tcnt % 8 == 0:
                            p.wait("tensor", [pTe_free, lxl])
                        tr = p.inc("tensor", lambda e, cch=cch, k=k, tcnt=tcnt: e.transpose(
                            pTe[:, (tcnt % 8) * 128:(tcnt % 8 + 1) * 128],
                            lnx[:, cch * 2176 + k * 128: cch * 2176 + (k + 1) * 128],
                            idb[:]), "tt")
                        tcnt += 1
                        if tcnt % 8 == 0:
                            kk0 = k - 7
                            p.wait("vector", tr)
                            ev = p.inc("vector", lambda e, cch=cch, kk0=kk0, sg2=sg2:
                                       e.tensor_copy(
                                lnx_fm.rearrange("p (k c) -> p k c", k=KC)
                                [:, kk0:kk0 + 8, sg2 * 512 + cch * 128:
                                 sg2 * 512 + (cch + 1) * 128],
                                pTe.rearrange("p (k c) -> p k c", k=8)[:, :, :]), "vv")
                            evs.append(ev)
                            pTe_free = ev
                lnx_ready = [lnx_ready, evs[-1]]
                lnx_free = evs[-1]
            p._tokd = tokd
            emit_branch(p, "e", be, 0, lnx_ready, dn_e)
        x3_prev = None
        xl_prev = None
        for i in range(NQT):
            p.wait("sync", [xl_prev, x3_prev, acc_chain])
            al = p.inc("sync", lambda e, i=i: e.dma_start(
                stage[1][:], acc_sp[i * 128:(i + 1) * 128, :]), "acl", 16)
            xld = p.inc("sync", lambda e, i=i: e.dma_start(
                stage[0][:], x2_sp[i]), "x2r", 16)
            p.wait("vector", al)
            p.drain("vector")
            p.inc("vector", lambda e: e.reduce_sum(st_d[:, 0:1], stage[1][:], axis=AX), "vv")
            p.wait("scalar", [al, pD_free])
            sq1 = p.inc("scalar", lambda e: e.activation(
                pD[:], stage[1][:], ACT.Square, accum_out=st_d[:, 1:2]), "ss")
            pD_free = sq1
            rstd = _ln_stats(p, st_d, [p.tok("vv"), sq1])
            p.wait("vector", [rstd, xld])
            p.drain("vector")
            p.op("vector", lambda e: e.tensor_scalar(
                stage[1][:], stage[1][:], st_d[:, 0:1], st_d[:, 5:6],
                OP.subtract, OP.mult))
            p.drain("vector")
            x3d = p.inc("vector", lambda e: e.tensor_add(
                stage[0][:], stage[0][:], stage[1][:]), "vv")
            xl_prev = x3d
            p.wait("sync", x3d)
            x3_prev = p.inc("sync", lambda e, i=i: e.dma_start(
                yout[i * 128:(i + 1) * 128, :], stage[0][:]), "out", 16)
        p.quiesce("sync")
        p.run()

    return nc


def kernel(**inputs):
    shared = _prep_shared(inputs)
    in_maps = []
    for core in range(8):
        b, half = core // 2, core % 2
        in_maps.append(_prep_core(inputs, shared, b, half))
    nc = _build_nc()
    res = run_bass_kernel_spmd(nc, in_maps, core_ids=list(range(8)))
    out = np.empty((B, T, C), dtype=np.float32)
    for core in range(8):
        b, half = core // 2, core % 2
        out[b, half * TLOC:(half + 1) * TLOC] = res.results[core]["yout"]
    return out


# version 13 (final)
# speedup vs baseline: 1.0697x; 1.0697x over previous
import contextlib
import numpy as np
import ml_dtypes
import concourse.bass as bass
import concourse.mybir as mybir
from concourse.bass_utils import run_bass_kernel_spmd

F32 = mybir.dt.float32
BF16 = mybir.dt.bfloat16
OP = mybir.AluOpType
ACT = mybir.ActivationFunctionType
AX = mybir.AxisListType.X

C = 2048
KC = 16
T = 4096
B = 4
TLOC = 2048
NH = 16
NKVH = 4
HD = 128
LKV = 512
BAND = 2560
NGLOB = 64
NKV = BAND + 2 * NGLOB
NKVC = NKV // 128
NKEY = 704
NQT = 16
H_E = 1408
HS = 2816
NE = 8
NEG = -1.0e30
GDELTA = 0.03
SCALE = float(1.0 / np.sqrt(np.float64(HD)))
EPS = 1e-5
RN = float(np.float32(1.0 / C))
BF = ml_dtypes.bfloat16
ENGINES = ("sync", "tensor", "vector", "scalar", "gpsimd")
DMA_FAMS = ("ci", "cf", "co", "wos", "xs0", "xs1", "l10", "l11", "ckl", "msk",
            "lt0", "lt1", "cq0", "cq1", "ys0", "ys1", "yl0", "yl1", "xq0",
            "xq1", "wo0", "wo1", "wo2", "wo3", "x2s", "l3s0", "l3s1", "b2b", "l3h", "x2r", "out", "lx0", "lx1",
            "lxe", "fs0", "fs1", "lnx", "lxr0", "lxr1", "acs", "aci", "acl",
            "wkv", "wol", "kup", "wq0", "wq1", "gat", "ug0", "ug1",
            "uu0", "uu1", "dnl")


def _rope_tables():
    try:
        import jax
        import jax.numpy as jnp
        with jax.default_device(jax.devices("cpu")[0]):
            freqs = 1.0 / 10000.0 ** (jnp.arange(0, HD, 2, dtype=jnp.float32) / HD)
            emb = jnp.arange(T, dtype=jnp.float32)[:, None] * freqs[None, :]
            cos_h = np.asarray(jnp.cos(emb))
            sin_h = np.asarray(jnp.sin(emb))
    except Exception:
        freqs = (1.0 / (10000.0 ** (np.arange(0, HD, 2, dtype=np.float32) /
                                    np.float32(HD)))).astype(np.float32)
        emb = (np.arange(T, dtype=np.float32)[:, None] * freqs[None, :]
               ).astype(np.float32)
        cos_h = np.cos(emb.astype(np.float64)).astype(np.float32)
        sin_h = np.sin(emb.astype(np.float64)).astype(np.float32)
    cos = np.concatenate([cos_h, cos_h], axis=1)
    sin = np.concatenate([sin_h, sin_h], axis=1)
    return cos, sin


_COS_SIN = _rope_tables()


def _kv_positions(half):
    q0 = half * TLOC
    band = np.clip(np.arange(q0 - 512, q0 + TLOC), 0, T - 1)
    glob = np.arange(NGLOB) * 64
    pad = np.zeros(NGLOB, dtype=np.int64)
    return np.concatenate([band, glob, pad])


def _build_xkv(x_b, half):
    q0 = half * TLOC
    out = np.zeros((NKV, C), dtype=np.float32)
    lo = q0 - 512
    src_lo = max(0, lo)
    out[src_lo - lo: BAND] = x_b[src_lo: q0 + TLOC]
    out[BAND:BAND + NGLOB] = x_b[np.arange(NGLOB) * 64]
    return out


def _build_masks(half):
    q0 = half * TLOC
    m = np.zeros((NQT, 128, NKEY), dtype=np.float32)
    r = np.arange(128)[:, None]
    j = np.arange(640)[None, :]
    univ = (r > j) & (j % 64 != 0)
    for i in range(NQT):
        mi = m[i]
        mi[:, :640][univ] = NEG
        pad = 512 - 128 * i - q0
        if pad > 0:
            mi[:, :pad] = NEG
        g_valid = min(max(0, (q0 + 128 * i - 512) // 64), NGLOB)
        mi[:, 640 + g_valid:] = NEG
    return np.ascontiguousarray(m.astype(BF))


def _fm(a):
    return np.ascontiguousarray(a.T)


def _prep_shared(inputs):
    d = {}
    d["wq_l"] = np.ascontiguousarray(
        np.asarray(inputs["L0_attn_w_q"], np.float32)
        .reshape(KC, 128, NH, 128).transpose(2, 1, 0, 3).reshape(NH, 128, C))
    d["wkvd"] = np.ascontiguousarray(np.asarray(inputs["L0_attn_w_kv_down"], np.float32))
    d["wkup"] = np.ascontiguousarray(np.asarray(inputs["L0_attn_w_k_up"], np.float32))
    d["wvup"] = np.ascontiguousarray(np.asarray(inputs["L0_attn_w_v_up"], np.float32))
    d["wo"] = np.ascontiguousarray(np.asarray(inputs["L0_attn_w_o"], np.float32))
    gT = np.asarray(inputs["L0_ffn_gate_w"], np.float32).T
    d["gate_l"] = np.ascontiguousarray(
        gT.reshape(KC, 128, NE).transpose(1, 0, 2).reshape(128, KC * NE))
    d["bias_bc"] = np.ascontiguousarray(
        np.broadcast_to(np.asarray(inputs["L0_ffn_gate_bias"], np.float32)[None, :],
                        (128, NE)))
    gwT = np.asarray(inputs["L0_ffn_shared_gw"], np.float32).T
    d["gw_l"] = np.ascontiguousarray(
        gwT.reshape(KC, 128, 44, 128).transpose(2, 1, 0, 3).reshape(44, 128, C))
    d["dwT"] = np.ascontiguousarray(np.asarray(inputs["L0_ffn_shared_dw"], np.float32).T)
    upT = np.asarray(inputs["L0_ffn_up_w"], np.float32).transpose(0, 2, 1)
    d["up_l"] = np.ascontiguousarray(
        upT.reshape(NE, KC, 128, 22, 128).transpose(0, 3, 2, 1, 4).reshape(NE, 22, 128, C))
    d["dnT"] = np.ascontiguousarray(
        np.asarray(inputs["L0_ffn_down_w"], np.float32).transpose(0, 2, 1))
    sel = np.zeros((NE, NE * 128), dtype=np.float32)
    for e in range(NE):
        sel[e, e * 128:(e + 1) * 128] = 1.0
    d["esel"] = sel
    d["identb"] = np.eye(128, dtype=np.float32).astype(BF)
    d["identf"] = np.eye(128, dtype=np.float32)
    d["onesf"] = np.ones((1, 128), dtype=np.float32).astype(BF)
    return d


def _prep_core(inputs, shared, b, half):
    cos, sin = _COS_SIN
    q0 = half * TLOC
    x_b = np.asarray(inputs["L0_attn_x"], np.float32)[b]
    d = dict(shared)
    d["xkv"] = _build_xkv(x_b, half)
    kvpos = _kv_positions(half)
    d["cosk"] = _fm(cos[kvpos])
    sk = sin[kvpos]
    d["sink"] = np.ascontiguousarray(
        np.concatenate([-sk[:, :64], sk[:, 64:]], axis=1).T)
    qpos = q0 + np.arange(TLOC)
    d["cosq"] = _fm(cos[qpos])
    sq = sin[qpos]
    d["sinq"] = np.ascontiguousarray(
        np.concatenate([-sq[:, :64], sq[:, 64:]], axis=1).T)
    d["maskq"] = _build_masks(half)
    return d


class _Ph:

    def __init__(self, nc, sems, cnt):
        self.nc = nc
        self.sems = sems
        self.cnt = cnt
        self.q = {e: [] for e in ENGINES}

    def op(self, eng, fn):
        self.q[eng].append(fn)

    def inc(self, eng, fn, sem, by=1):
        s = self.sems[sem]

        def g(e, fn=fn, s=s, by=by):
            fn(e).then_inc(s, by)
        self.q[eng].append(g)
        self.cnt[sem] += by
        return (sem, self.cnt[sem])

    def wait(self, eng, tok):
        if tok is None:
            return
        if isinstance(tok, list):
            best = {}

            def walk(t):
                if t is None:
                    return
                if isinstance(t, list):
                    for t2 in t:
                        walk(t2)
                else:
                    best[t[0]] = max(best.get(t[0], 0), t[1])
            walk(tok)
            for s, v in best.items():
                self.wait(eng, (s, v))
            return
        sem, val = tok
        s = self.sems[sem]

        def g(e, s=s, val=val):
            e.wait_ge(s, val)
        self.q[eng].append(g)

    def drain(self, eng):
        self.q[eng].append(lambda e: e.drain())

    def tok(self, sem):
        return (sem, self.cnt[sem]) if self.cnt[sem] > 0 else None

    def quiesce(self, eng="sync"):
        for name in DMA_FAMS:
            if self.cnt.get(name, 0) > 0:
                self.wait(eng, (name, self.cnt[name]))

    def run(self):
        with self.nc.Block() as blk:
            for eng in ENGINES:
                if self.q[eng]:
                    ops = self.q[eng]

                    def body(e, ops=ops):
                        for f in ops:
                            f(e)
                    getattr(blk, eng)(body)


def _ln_stats(p, st, sumsq_done, veng="vector", seng="scalar"):
    p.wait(veng, sumsq_done)
    p.drain(veng)
    p.op(veng, lambda e: e.tensor_scalar(st[:, 0:1], st[:, 0:1], RN, None, OP.mult))
    p.drain(veng)
    p.op(veng, lambda e: e.tensor_scalar(st[:, 2:3], st[:, 1:2], RN, EPS, OP.mult, OP.add))
    p.drain(veng)
    p.op(veng, lambda e: e.tensor_tensor(st[:, 3:4], st[:, 0:1], st[:, 0:1], OP.mult))
    p.drain(veng)
    vv = p.inc(veng, lambda e: e.tensor_sub(st[:, 2:3], st[:, 2:3], st[:, 3:4]), "vv")
    p.wait(seng, vv)
    p.op(seng, lambda e: e.activation(st[:, 4:5], st[:, 2:3], ACT.Ln))
    p.drain(seng)
    rstd = p.inc(seng, lambda e: e.activation(
        st[:, 5:6], st[:, 4:5], ACT.Exp, scale=-0.5), "ss")
    return rstd


def _build_nc():
    nc = bass.Bass("TRN2", target_bir_lowering=False)
    es = contextlib.ExitStack()
    nc._keepalive = es

    def din(name, shape, dt=F32):
        return nc.dram_tensor(name, shape, dt, kind="ExternalInput").ap()
    xkv = din("xkv", [NKV, C])
    wq_l = din("wq_l", [NH, 128, C])
    wkvd = din("wkvd", [C, LKV])
    wkup = din("wkup", [LKV, LKV])
    wvup = din("wvup", [LKV, LKV])
    wo = din("wo", [C, C])
    gate_l = din("gate_l", [128, KC * NE])
    bias_bc = din("bias_bc", [128, NE])
    gw_l = din("gw_l", [44, 128, C])
    dwT = din("dwT", [HS, C])
    up_l = din("up_l", [NE, 22, 128, C])
    dnT = din("dnT", [NE, H_E, C])
    cosq = din("cosq", [128, TLOC])
    sinq = din("sinq", [128, TLOC])
    cosk = din("cosk", [128, NKV])
    sink = din("sink", [128, NKV])
    maskq = din("maskq", [NQT, 128, NKEY], BF16)
    identb = din("identb", [128, 128], BF16)
    identf = din("identf", [128, 128])
    onesf = din("onesf", [1, 128], BF16)
    esel = din("esel", [NE, NE * 128])
    yout = nc.dram_tensor("yout", [TLOC, C], F32, kind="ExternalOutput").ap()
    wo_sp = nc.dram_tensor("wo_sp", [KC, 128, C], BF16).ap()
    ln1T_sp = nc.dram_tensor("ln1T_sp", [KC, 128, TLOC], BF16).ap()
    y_sp = nc.dram_tensor("y_sp", [NQT, 128, C], BF16).ap()
    x2_sp = nc.dram_tensor("x2_sp", [NQT, 128, C], F32).ap()
    ln3T_sp = nc.dram_tensor("ln3T_sp", [KC, 128, TLOC], BF16).ap()
    lx_sp = nc.dram_tensor("lx_sp", [NQT, 128, 2176], BF16).ap()
    lxe_sp = [nc.dram_tensor(f"lxe_sp{e}", [2048, 2176], BF16).ap()
              for e in range(NE)]
    acc_sp = nc.dram_tensor("acc_sp", [NQT * 128, C], F32).ap()

    sems = {}
    cnt = {}
    for s in ("tt", "vv", "ss", "gpc") + DMA_FAMS:
        sems[s] = es.enter_context(nc.semaphore(s))
        cnt[s] = 0

    idb = es.enter_context(nc.sbuf_tensor("idb", [128, 128], BF16))
    idf = es.enter_context(nc.sbuf_tensor("idf", [128, 128], F32))
    ones_sb = es.enter_context(nc.sbuf_tensor("ones_sb", [1, 128], BF16))
    weT = es.enter_context(nc.sbuf_tensor("weT", [NE, TLOC], F32))
    gweT = es.enter_context(nc.sbuf_tensor("gweT", [NE, TLOC], BF16))

    with contextlib.ExitStack() as psAB:
        Kp = psAB.enter_context(nc.sbuf_tensor("Kp", [128, NKVH * NKV], BF16))
        Vt = psAB.enter_context(nc.sbuf_tensor("Vt", [128, NKVC * 512], BF16))

        with contextlib.ExitStack() as psA:
            def sba(name, shape, dt):
                return psA.enter_context(nc.sbuf_tensor(name, shape, dt))
            ckv_sb = sba("ckv_sb", [128, 4 * NKV], BF16)
            pT = [psA.enter_context(nc.psum_tensor(f"pTa{i}", [128, 1024], BF16))
                  for i in range(2)]
            pA = [psA.enter_context(nc.psum_tensor(f"pAa{i}", [128, 512], F32))
                  for i in range(2)]

            with contextlib.ExitStack() as ps1:
                def sb1(name, shape, dt):
                    return ps1.enter_context(nc.sbuf_tensor(name, shape, dt))
                xs = [sb1(f"xs{i}", [128, C], F32) for i in range(2)]
                sq_scr = sb1("sq_scr", [128, C], F32)
                lnbf = [sb1(f"lnbf{i}", [128, C], BF16) for i in range(2)]
                st_a = sb1("st_a", [128, 8], F32)
                ln1Tb = [sb1(f"ln1Tb{i}", [128, KC * 512], BF16) for i in range(2)]
                wkvd_sb = sb1("wkvd_sb", [128, KC * 512], BF16)
                wostage = sb1("wostage", [128, C], BF16)

                p = _Ph(nc, sems, cnt)
                ld_idb = p.inc("sync", lambda e: e.dma_start(idb[:], identb[:]), "ci", 16)
                p.inc("sync", lambda e: e.dma_start(idf[:], identf[:]), "cf", 16)
                p.inc("sync", lambda e: e.dma_start(ones_sb[:], onesf[:]), "co", 16)
                ld_wkvd = None
                for k in range(KC):
                    ld_wkvd = p.inc("gpsimd", lambda e, k=k: e.dma_start(
                        wkvd_sb[:, k * 512:(k + 1) * 512],
                        wkvd[k * 128:(k + 1) * 128, :]), "wkv", 16)
                last_wo_store = None
                for k in range(KC):
                    p.wait("gpsimd", last_wo_store)
                    tok = p.inc("gpsimd", lambda e, k=k: e.dma_start(
                        wostage[:], wo[k * 128:(k + 1) * 128, :]), "wol", 16)
                    p.wait("sync", tok)
                    last_wo_store = p.inc("sync", lambda e, k=k: e.dma_start(
                        wo_sp[k], wostage[:]), "wos", 16)

                xs_free = [None, None]
                lnbf_free = [None, None]
                ln1Tb_free = [None, None]
                pA_free = [None, None]
                pT_free = [None, None]
                for cc in range(NKVC):
                    par = cc % 2
                    g, c4 = cc // 4, cc % 4
                    gpar = g % 2
                    p.wait("sync", xs_free[par])
                    ld = p.inc("sync", lambda e, cc=cc, par=par: e.dma_start(
                        xs[par][:], xkv[cc * 128:(cc + 1) * 128, :]), f"xs{par}", 16)
                    p.wait("vector", ld)
                    p.drain("vector")
                    p.inc("vector", lambda e, par=par: e.reduce_sum(
                        st_a[:, 0:1], xs[par][:], axis=AX), "vv")
                    p.wait("scalar", ld)
                    sqs = p.inc("scalar", lambda e, par=par: e.activation(
                        sq_scr[:], xs[par][:], ACT.Square, accum_out=st_a[:, 1:2]), "ss")
                    rstd = _ln_stats(p, st_a, [p.tok("vv"), sqs])
                    p.wait("vector", [rstd, lnbf_free[par]])
                    norm = p.inc("vector", lambda e, par=par: e.tensor_scalar(
                        lnbf[par][:], xs[par][:], st_a[:, 0:1], st_a[:, 5:6],
                        OP.subtract, OP.mult), "vv")
                    xs_free[par] = norm
                    p.wait("tensor", [norm, ld_idb, pT_free[0], pT_free[1]])
                    tr = None
                    for k in range(KC):
                        bank, col = k // 8, (k % 8) * 128
                        tr = p.inc("tensor", lambda e, par=par, bank=bank, col=col, k=k:
                                   e.transpose(pT[bank][:, col:col + 128],
                                               lnbf[par][:, k * 128:(k + 1) * 128],
                                               idb[:]), "tt")
                    ev = []
                    for bank in range(2):
                        p.wait("vector", [tr] + ([ln1Tb_free[gpar]] if (c4 == 0 and bank == 0) else []))
                        ev.append(p.inc("vector", lambda e, bank=bank, gpar=gpar, c4=c4:
                                        e.tensor_copy(
                            ln1Tb[gpar].rearrange("p (k c) -> p k c", k=KC)
                            [:, bank * 8:(bank + 1) * 8, c4 * 128:(c4 + 1) * 128],
                            pT[bank].rearrange("p (k c) -> p k c", k=8)[:, :, :]), "vv"))
                    lnbf_free[par] = ev
                    pT_free = [ev[0], ev[1]]
                    last_in_group = (c4 == 3) or (cc == NKVC - 1)
                    if last_in_group:
                        n = (c4 + 1) * 128
                        sp = []
                        if 1 <= g <= 4:
                            for k in range(KC):
                                p.wait("sync", ev)
                                sp.append(p.inc("sync", lambda e, k=k, g=g, gpar=gpar:
                                                e.dma_start(
                                    ln1T_sp[k][:, (g - 1) * 512: g * 512],
                                    ln1Tb[gpar][:, k * 512:(k + 1) * 512]), f"l1{gpar}", 16))
                        mm_done = []
                        for m in range(4):
                            mp = m % 2
                            p.wait("tensor", [pA_free[mp], ev, ld_wkvd])
                            mm = None
                            for k in range(KC):
                                mm = p.inc("tensor", lambda e, m=m, k=k, gpar=gpar, n=n,
                                           mp=mp: e.matmul(
                                    pA[mp][:, 0:n],
                                    wkvd_sb[:, k * 512 + m * 128: k * 512 + (m + 1) * 128],
                                    ln1Tb[gpar][:, k * 512: k * 512 + n],
                                    start=(k == 0), stop=(k == KC - 1)), "tt")
                            p.wait("vector", mm)
                            evk = p.inc("vector", lambda e, m=m, g=g, n=n, mp=mp:
                                        e.tensor_copy(
                                ckv_sb[:, m * NKV + g * 512: m * NKV + g * 512 + n],
                                pA[mp][:, 0:n]), "vv")
                            pA_free[mp] = evk
                            mm_done.append(evk)
                        ln1Tb_free[gpar] = mm_done + sp
                p.quiesce("sync")
                p.run()

            with contextlib.ExitStack() as ps2:
                def sb2(name, shape, dt):
                    return ps2.enter_context(nc.sbuf_tensor(name, shape, dt))
                kup_sb = sb2("kup_sb", [128, 4 * 512], BF16)
                vup_sb = sb2("vup_sb", [128, 4 * 512], BF16)
                cosk_sb = sb2("cosk_sb", [128, NKV], F32)
                sink_sb = sb2("sink_sb", [128, NKV], F32)
                vfm = sb2("vfm", [128, 4 * NKV], BF16)
                rt1 = sb2("rt1", [128, 512], F32)
                rt2 = sb2("rt2", [128, 512], F32)

                p = _Ph(nc, sems, cnt)
                ld_kup = None
                for k in range(4):
                    p.inc("gpsimd", lambda e, k=k: e.dma_start(
                        kup_sb[:, k * 512:(k + 1) * 512],
                        wkup[k * 128:(k + 1) * 128, :]), "kup", 16)
                    ld_kup = p.inc("gpsimd", lambda e, k=k: e.dma_start(
                        vup_sb[:, k * 512:(k + 1) * 512],
                        wvup[k * 128:(k + 1) * 128, :]), "kup", 16)
                ldc = p.inc("sync", lambda e: e.dma_start(cosk_sb[:], cosk[:]), "ckl", 16)
                lds = p.inc("sync", lambda e: e.dma_start(sink_sb[:], sink[:]), "ckl", 16)
                pA_free = [None, None]
                pT_free = [None, None]
                vfm_ready = {}
                for g in range(6):
                    n = 512 if g < 5 else 128
                    for kv in range(NKVH):
                        mp = kv % 2
                        p.wait("tensor", [pA_free[mp], ld_kup])
                        mm = None
                        for k in range(4):
                            mm = p.inc("tensor", lambda e, k=k, kv=kv, g=g, n=n, mp=mp:
                                       e.matmul(
                                pA[mp][:, 0:n],
                                kup_sb[:, k * 512 + kv * 128: k * 512 + (kv + 1) * 128],
                                ckv_sb[:, k * NKV + g * 512: k * NKV + g * 512 + n],
                                start=(k == 0), stop=(k == 3)), "tt")
                        p.wait("vector", [mm, ldc, lds])
                        p.drain("vector")
                        p.op("vector", lambda e, g=g, n=n, mp=mp: e.tensor_mul(
                            rt1[0:64, 0:n], pA[mp][64:128, 0:n],
                            sink_sb[0:64, g * 512:g * 512 + n]))
                        p.op("vector", lambda e, g=g, n=n, mp=mp: e.tensor_mul(
                            rt1[64:128, 0:n], pA[mp][0:64, 0:n],
                            sink_sb[64:128, g * 512:g * 512 + n]))
                        p.op("vector", lambda e, g=g, n=n, mp=mp: e.tensor_mul(
                            rt2[:, 0:n], pA[mp][:, 0:n],
                            cosk_sb[:, g * 512:g * 512 + n]))
                        p.drain("vector")
                        kev = p.inc("vector", lambda e, kv=kv, g=g, n=n: e.tensor_add(
                            Kp[:, kv * NKV + g * 512: kv * NKV + g * 512 + n],
                            rt2[:, 0:n], rt1[:, 0:n]), "vv")
                        pA_free[mp] = kev
                        p.wait("tensor", pA_free[mp])
                        mm = None
                        for k in range(4):
                            mm = p.inc("tensor", lambda e, k=k, kv=kv, g=g, n=n, mp=mp:
                                       e.matmul(
                                pA[mp][:, 0:n],
                                vup_sb[:, k * 512 + kv * 128: k * 512 + (kv + 1) * 128],
                                ckv_sb[:, k * NKV + g * 512: k * NKV + g * 512 + n],
                                start=(k == 0), stop=(k == 3)), "tt")
                        p.wait("vector", mm)
                        vev = p.inc("vector", lambda e, kv=kv, g=g, n=n, mp=mp:
                                    e.tensor_copy(
                            vfm[:, kv * NKV + g * 512: kv * NKV + g * 512 + n],
                            pA[mp][:, 0:n]), "vv")
                        pA_free[mp] = vev
                        vfm_ready[(g, kv)] = vev
                    npc = n // 128
                    for pc2 in range((npc + 1) // 2):
                        bank = pc2 % 2
                        p.wait("tensor", [pT_free[bank]] +
                               [vfm_ready[(g, kv)] for kv in range(NKVH)])
                        tr = None
                        for pc_i in range(2):
                            if pc2 * 2 + pc_i >= npc:
                                continue
                            pc = g * 4 + pc2 * 2 + pc_i
                            for kv in range(NKVH):
                                tr = p.inc("tensor", lambda e, pc=pc, kv=kv, bank=bank,
                                           pc_i=pc_i: e.transpose(
                                    pT[bank][:, pc_i * 512 + kv * 128:
                                             pc_i * 512 + (kv + 1) * 128],
                                    vfm[:, kv * NKV + pc * 128: kv * NKV + (pc + 1) * 128],
                                    idb[:]), "tt")
                        p.wait("vector", tr)
                        nn = min(1024, (npc - pc2 * 2) * 512)
                        vt = p.inc("vector", lambda e, g=g, pc2=pc2, bank=bank, nn=nn:
                                   e.tensor_copy(
                            Vt[:, (g * 4 + pc2 * 2) * 512: (g * 4 + pc2 * 2) * 512 + nn],
                            pT[bank][:, 0:nn]), "vv")
                        pT_free[bank] = vt
                p.quiesce("sync")
                p.run()

        with contextlib.ExitStack() as ps:
            def sb(name, shape, dt):
                return ps.enter_context(nc.sbuf_tensor(name, shape, dt))
            q_sb = sb("q_sb", [128, NH * 512], BF16)
            ln1Tt = [sb(f"ln1Tt{i}", [128, KC * 512], BF16) for i in range(2)]
            masks_sb = sb("masks_sb", [128, NQT * NKEY], BF16)
            cq = [sb(f"cq{i}", [128, 512], F32) for i in range(2)]
            sq_ = [sb(f"sq_{i}", [128, 512], F32) for i in range(2)]
            wqb = [sb(f"wqb{i}", [128, C], BF16) for i in range(2)]
            NSC = 6
            NPR = 12
            sc_sb = [sb(f"sc_sb{i}", [128, NKEY], F32) for i in range(NSC)]
            e_scr = [sb("e_scr0", [128, NKEY], F32)]
            pr_bf = [sb(f"pr_bf{i}", [128, NKEY], BF16) for i in range(NPR)]
            prT = sb("prT", [128, 12 * 768], BF16)
            y_sb = [sb(f"y_sb{i}", [128, C], BF16) for i in range(2)]
            st_b = [sb(f"st_b{i}", [128, 4], F32) for i in range(NSC)]
            rt1 = sb("rt1b", [128, 512], F32)
            rt2 = sb("rt2b", [128, 512], F32)
            pS_a = [ps.enter_context(nc.psum_tensor(f"pSa{i}", [128, 512], F32))
                    for i in range(2)]
            pS_b = [ps.enter_context(nc.psum_tensor(f"pSb{i}", [128, 512], F32))
                    for i in range(2)]
            pPT = [ps.enter_context(nc.psum_tensor(f"pPT{i}", [128, 1024], BF16))
                   for i in range(2)]
            pY = [ps.enter_context(nc.psum_tensor(f"pY{i}", [128, 512], F32))
                  for i in range(2)]

            p = _Ph(nc, sems, cnt)
            mld = None
            for i in range(NQT):
                mld = p.inc("sync", lambda e, i=i: e.dma_start(
                    masks_sb[:, i * NKEY:(i + 1) * NKEY], maskq[i]), "msk", 16)
            ln1_free = [None, None]
            cq_free = [None, None]
            wqb_free = [None, None]
            pSa_free = [None, None]
            pSb_free = [None, None]
            pPT_free = [None, None]
            pY_free = [None, None]
            sc_free = [None] * NSC
            pr_free = [None] * NPR
            y_free = [None, None]
            prTset_free = [None, None, None]
            ctr = {"h": 0, "u": 0, "tr": 0}
            pend = []

            def softmax_unit(i, kvg, rope_tok):
                kv = kvg
                heads = []
                for hh in range(4):
                    h = kvg * 4 + hh
                    hp = ctr["h"] % 2
                    ssl = ctr["h"] % NSC
                    psl = ctr["h"] % NPR
                    esl = 0
                    ctr["h"] += 1
                    qsl = (h * 512 + (i % 4) * 128, 128)
                    p.wait("tensor", [pSa_free[hp], pSb_free[hp]] +
                           (rope_tok if (hh == 0 and rope_tok) else []))
                    p.inc("tensor", lambda e, qsl=qsl, i=i, kv=kv, hp=hp: e.matmul(
                        pS_a[hp][:], q_sb[:, qsl[0]:qsl[0] + 128],
                        Kp[:, kv * NKV + i * 128: kv * NKV + i * 128 + 512],
                        start=True, stop=True), "tt")
                    p.inc("tensor", lambda e, qsl=qsl, i=i, kv=kv, hp=hp: e.matmul(
                        pS_b[hp][:, 0:128], q_sb[:, qsl[0]:qsl[0] + 128],
                        Kp[:, kv * NKV + i * 128 + 512: kv * NKV + i * 128 + 640],
                        start=True, stop=True), "tt")
                    s3 = p.inc("tensor", lambda e, qsl=qsl, kv=kv, hp=hp: e.matmul(
                        pS_b[hp][:, 128:192], q_sb[:, qsl[0]:qsl[0] + 128],
                        Kp[:, kv * NKV + BAND: kv * NKV + BAND + NGLOB],
                        start=True, stop=True), "tt")
                    p.wait("vector", [s3, mld, sc_free[ssl]])
                    p.drain("vector")
                    p.op("vector", lambda e, i=i, hp=hp, ssl=ssl: e.scalar_tensor_tensor(
                        sc_sb[ssl][:, 0:512], pS_a[hp][:], SCALE,
                        masks_sb[:, i * NKEY: i * NKEY + 512], OP.mult, OP.add))
                    ev1 = p.inc("vector", lambda e, i=i, hp=hp, ssl=ssl:
                                e.scalar_tensor_tensor(
                        sc_sb[ssl][:, 512:704], pS_b[hp][:, 0:192], SCALE,
                        masks_sb[:, i * NKEY + 512: i * NKEY + 704],
                        OP.mult, OP.add), "vv")
                    pSa_free[hp] = ev1
                    pSb_free[hp] = ev1
                    p.drain("vector")
                    mx = p.inc("vector", lambda e, ssl=ssl: e.reduce_max(
                        st_b[ssl][:, 0:1], sc_sb[ssl][:], axis=AX, negate=True), "vv")
                    p.wait("scalar", mx)
                    p.inc("scalar", lambda e, ssl=ssl, esl=esl: e.activation(
                        e_scr[esl][:], sc_sb[ssl][:], ACT.Exp, bias=st_b[ssl][:, 0:1],
                        accum_out=st_b[ssl][:, 1:2]), "ss")
                    p.drain("scalar")
                    p.op("scalar", lambda e, ssl=ssl: e.activation(
                        st_b[ssl][:, 2:3], st_b[ssl][:, 1:2], ACT.Ln))
                    p.drain("scalar")
                    p.op("scalar", lambda e, ssl=ssl: e.activation(
                        st_b[ssl][:, 3:4], st_b[ssl][:, 2:3], ACT.Identity,
                        scale=-1.0, bias=st_b[ssl][:, 0:1]))
                    p.drain("scalar")
                    p.wait("scalar", pr_free[psl])
                    e3 = p.inc("scalar", lambda e, ssl=ssl, psl=psl: e.activation(
                        pr_bf[psl][:], sc_sb[ssl][:], ACT.Exp,
                        bias=st_b[ssl][:, 3:4]), "ss")
                    sc_free[ssl] = e3
                    heads.append((psl, e3))
                return (i, kvg, heads)

            def flush_unit(u):
                i, kvg, heads = u
                kv = kvg
                uset = ctr["u"] % 3
                ctr["u"] += 1
                for hh, (psl, e3) in enumerate(heads):
                    pp = ctr["tr"] % 2
                    ctr["tr"] += 1
                    p.wait("tensor", [e3, pPT_free[pp]])
                    for c2 in range(5):
                        p.inc("tensor", lambda e, c2=c2, psl=psl, pp=pp: e.transpose(
                            pPT[pp][:, c2 * 128:(c2 + 1) * 128],
                            pr_bf[psl][:, c2 * 128:(c2 + 1) * 128], idb[:]), "tt")
                    tr = p.inc("tensor", lambda e, psl=psl, pp=pp: e.transpose(
                        pPT[pp][0:64, 640:768],
                        pr_bf[psl][:, 640:704], idb[:]), "tt")
                    pr_free[psl] = tr
                    p.wait("vector", [tr, prTset_free[uset]])
                    p.inc("vector", lambda e, hh=hh, pp=pp, uset=uset: e.tensor_copy(
                        prT[:, uset * 3072 + hh * 768: uset * 3072 + hh * 768 + 640],
                        pPT[pp][:, 0:640]), "vv")
                    pe2 = p.inc("vector", lambda e, hh=hh, pp=pp, uset=uset: e.tensor_copy(
                        prT[0:64, uset * 3072 + hh * 768 + 640:
                            uset * 3072 + hh * 768 + 768],
                        pPT[pp][0:64, 640:768]), "vv")
                    pPT_free[pp] = pe2
                    heads[hh] = (psl, e3, pe2)
                yp = ctr["u"] % 2
                p.wait("tensor", [h[2] for h in heads] + [pY_free[yp]])
                mm = None
                first = True
                for c2 in range(6):
                    if c2 < 5:
                        vof, kof, kdim = (i + c2) * 512 + kv * 128, c2 * 128, 128
                    else:
                        vof, kof, kdim = 20 * 512 + kv * 128, 640, 64
                    for hh in range(4):
                        mm = p.inc("tensor",
                                   lambda e, vof=vof, kof=kof, kdim=kdim, hh=hh,
                                   yp=yp, first=first, c2=c2, uset=uset: e.matmul(
                            pY[yp][:, hh * 128:(hh + 1) * 128],
                            Vt[0:kdim, vof:vof + 128],
                            prT[0:kdim, uset * 3072 + hh * 768 + kof:
                                uset * 3072 + hh * 768 + kof + 128],
                            start=first, stop=(c2 == 5 and hh == 3),
                            skip_group_check=True), "tt")
                        first = False
                prTset_free[uset] = mm
                yqp = (i % 4) % 2
                p.wait("vector", [mm] + ([y_free[yqp]] if kvg == 0 else []))
                yev = p.inc("vector", lambda e, kvg=kvg, yp=yp, yqp=yqp: e.tensor_copy(
                    y_sb[yqp][:, kvg * 512:(kvg + 1) * 512], pY[yp][:]), "vv")
                pY_free[yp] = yev
                if kvg == 3:
                    p.wait("sync", yev)
                    y_free[yqp] = p.inc("sync", lambda e, i=i, yqp=yqp: e.dma_start(
                        y_sp[i], y_sb[yqp][:]), f"ys{yqp}", 16)

            for tt in range(4):
                tp = tt % 2
                p.wait("sync", ln1_free[tp])
                lnld = None
                for k in range(KC):
                    lnld = p.inc("sync", lambda e, k=k, tt=tt, tp=tp: e.dma_start(
                        ln1Tt[tp][:, k * 512:(k + 1) * 512],
                        ln1T_sp[k][:, tt * 512:(tt + 1) * 512]), f"lt{tp}", 16)
                p.wait("sync", cq_free[tp])
                cld = p.inc("sync", lambda e, tt=tt, tp=tp: e.dma_start(
                    cq[tp][:], cosq[:, tt * 512:(tt + 1) * 512]), f"cq{tp}", 16)
                cld2 = p.inc("sync", lambda e, tt=tt, tp=tp: e.dma_start(
                    sq_[tp][:], sinq[:, tt * 512:(tt + 1) * 512]), f"cq{tp}", 16)
                rope_done = []
                for m in range(NH):
                    mp = m % 2
                    p.wait("gpsimd", wqb_free[mp])
                    wld = p.inc("gpsimd", lambda e, m=m, mp=mp: e.dma_start(
                        wqb[mp][:], wq_l[m]), f"wq{mp}", 16)
                    p.wait("tensor", [wld, lnld, pSa_free[mp]])
                    mm = None
                    for k in range(KC):
                        mm = p.inc("tensor", lambda e, m=m, k=k, tp=tp, mp=mp: e.matmul(
                            pS_a[mp][:],
                            wqb[mp][:, k * 128:(k + 1) * 128],
                            ln1Tt[tp][:, k * 512:(k + 1) * 512],
                            start=(k == 0), stop=(k == KC - 1)), "tt")
                    wqb_free[mp] = mm
                    p.wait("vector", [mm, cld, cld2])
                    p.drain("vector")
                    p.op("vector", lambda e, tp=tp, mp=mp: e.tensor_mul(
                        rt1[0:64, :], pS_a[mp][64:128, :], sq_[tp][0:64, :]))
                    p.op("vector", lambda e, tp=tp, mp=mp: e.tensor_mul(
                        rt1[64:128, :], pS_a[mp][0:64, :], sq_[tp][64:128, :]))
                    p.op("vector", lambda e, tp=tp, mp=mp: e.tensor_mul(
                        rt2[:, :], pS_a[mp][:, :], cq[tp][:, :]))
                    p.drain("vector")
                    rp = p.inc("vector", lambda e, m=m: e.tensor_add(
                        q_sb[:, m * 512:(m + 1) * 512], rt2[:], rt1[:]), "vv")
                    pSa_free[mp] = rp
                    rope_done.append(rp)
                for qt in range(4):
                    i = tt * 4 + qt
                    for kvg in range(4):
                        u = softmax_unit(i, kvg,
                                         rope_done if (qt == 0 and kvg == 0) else None)
                        pend.append(u)
                        if len(pend) > 2:
                            flush_unit(pend.pop(0))
                ln1_free[tp] = p.tok("tt")
                cq_free[tp] = p.tok("vv")
            while pend:
                flush_unit(pend.pop(0))
            p.quiesce("sync")
            p.run()

    with contextlib.ExitStack() as ps:
        def sb(name, shape, dt):
            return ps.enter_context(nc.sbuf_tensor(name, shape, dt))
        ysb = [sb(f"ysb{i}", [128, C], BF16) for i in range(2)]
        xq = [sb(f"xq{i}", [128, C], F32) for i in range(2)]
        wob = [sb(f"wob{i}", [128, C], BF16) for i in range(4)]
        att_sb = sb("att_sb", [128, C], F32)
        sq_scr = sb("sq_scrb", [128, C], F32)
        x2_sb = sb("x2_sb", [128, C], F32)
        ln3_bf = [sb(f"ln3_bf{i}", [128, C], BF16) for i in range(2)]
        ln3Tt = [sb(f"ln3Tt{i}", [128, KC * 128], BF16) for i in range(2)]
        gate_sb = sb("gate_sb", [128, KC * NE], BF16)
        bias_sb = sb("bias_sb", [128, NE], F32)
        s_sb = sb("s_sb", [128, NE], F32)
        sbb = sb("sbb", [128, NE], F32)
        eqm = sb("eqm", [128, NE], F32)
        wsel = sb("wsel", [128, NE], F32)
        st_c = sb("st_c", [128, 8], F32)
        st_w = sb("st_w", [128, 4], F32)
        gse = sb("gse", [128, NE], F32)
        we_ext = sb("we_ext", [128, 128], BF16)
        iota_p = sb("iota_p", [128, 1], mybir.dt.int32)
        pW = ps.enter_context(nc.psum_tensor("pW", [128, 2048], F32))
        pTc = ps.enter_context(nc.psum_tensor("pTc", [128, 1024], BF16))
        pG = ps.enter_context(nc.psum_tensor("pG", [128, NE], F32))
        pX = ps.enter_context(nc.psum_tensor("pX", [128, 512], F32))

        p = _Ph(nc, sems, cnt)
        gld = p.inc("gpsimd", lambda e: e.dma_start(gate_sb[:], gate_l[:]), "gat", 16)
        bld = p.inc("sync", lambda e: e.dma_start(bias_sb[:], bias_bc[:]), "b2b", 16)
        iog = p.inc("gpsimd", lambda e: e.iota(iota_p[:], pattern=[[0, 1]], base=0,
                                               channel_multiplier=1), "gpc")
        p.wait("vector", iog)
        p.op("vector", lambda e: e.tensor_copy(we_ext[:, 9:10], iota_p[:]))
        p.drain("vector")
        p.op("vector", lambda e: e.memset(we_ext[:, 0:9], 0.0))
        p.drain("vector")
        wem = p.inc("vector", lambda e: e.memset(we_ext[:, 10:128], 0.0), "vv")
        p.drain("vector")
        y_free = [None, None]
        xq_free = [None, None]
        wob_free = [None] * 4
        st = {"pW": None, "pTc": None, "pG": None, "pX": None, "att": None,
              "x2": None, "ln3bf": [None, None], "ln3Tt": [None, None],
              "lxd": [None, None], "wex": None}
        stage2_in = [None] * NQT

        def stage1(i):
            ip = i % 2
            p.wait("scalar", y_free[ip])
            yld = p.inc("scalar", lambda e, i=i, ip=ip: e.dma_start(
                ysb[ip][:], y_sp[i]), f"yl{ip}", 16)
            p.wait("scalar", xq_free[ip])
            xld = p.inc("scalar", lambda e, i=i, ip=ip: e.dma_start(
                xq[ip][:], xkv[512 + i * 128: 512 + (i + 1) * 128, :]), f"xq{ip}", 16)
            p.wait("tensor", st["pW"])
            mm = None
            for k in range(KC):
                kp = k % 4
                p.wait("gpsimd", wob_free[kp])
                wld = p.inc("gpsimd", lambda e, k=k, kp=kp: e.dma_start(
                    wob[kp][:], wo_sp[k]), f"wo{kp}", 16)
                p.wait("tensor", [wld] + ([yld] if k == 0 else []))
                for n in range(4):
                    mm = p.inc("tensor", lambda e, k=k, n=n, ip=ip, kp=kp: e.matmul(
                        pW[:, n * 512:(n + 1) * 512],
                        ysb[ip][:, k * 128:(k + 1) * 128],
                        wob[kp][:, n * 512:(n + 1) * 512],
                        start=(k == 0), stop=(k == KC - 1)), "tt")
                wob_free[kp] = mm
            y_free[ip] = mm
            p.wait("scalar", [mm, st["att"], st.get("stc")])
            p.inc("scalar", lambda e: e.activation(
                att_sb[:], pW[:], ACT.Copy, accum_out=st_c[:, 0:1]), "ss")
            p.drain("scalar")
            sq1 = p.inc("scalar", lambda e: e.activation(
                sq_scr[:], pW[:], ACT.Square, accum_out=st_c[:, 1:2]), "ss")
            st["pW"] = sq1
            rstd = _ln_stats(p, st_c, sq1)
            p.wait("vector", [rstd, xld, st["x2"]])
            p.drain("vector")
            p.op("vector", lambda e: e.tensor_scalar(
                sq_scr[:], att_sb[:], st_c[:, 0:1], st_c[:, 5:6], OP.subtract, OP.mult))
            p.drain("vector")
            x2d = p.inc("vector", lambda e, ip=ip: e.tensor_add(
                x2_sb[:], xq[ip][:], sq_scr[:]), "vv")
            st["att"] = x2d
            xq_free[ip] = x2d
            p.wait("sync", x2d)
            x2st = p.inc("sync", lambda e, i=i: e.dma_start(x2_sp[i], x2_sb[:]), "x2s", 16)
            p.drain("vector")
            p.inc("vector", lambda e: e.reduce_sum(st_c[:, 0:1], x2_sb[:], axis=AX), "vv")
            p.wait("scalar", x2d)
            sq3 = p.inc("scalar", lambda e: e.activation(
                sq_scr[:], x2_sb[:], ACT.Square, accum_out=st_c[:, 1:2]), "ss")
            rstd3 = _ln_stats(p, st_c, [p.tok("vv"), sq3])
            p.wait("vector", [rstd3, st["ln3bf"][ip]])
            ln3d = p.inc("vector", lambda e, ip=ip: e.tensor_scalar(
                ln3_bf[ip][:], x2_sb[:], st_c[:, 0:1], st_c[:, 5:6],
                OP.subtract, OP.mult), "vv")
            st["x2"] = [ln3d, x2st]
            st["stc"] = ln3d
            p.wait("sync", ln3d)
            lxd = p.inc("sync", lambda e, i=i, ip=ip: e.dma_start(
                lx_sp[i][:, 0:2048], ln3_bf[ip][:]), f"lx{ip}", 16)
            st["lxd"][ip] = lxd
            stage2_in[i] = ln3d

        def stage2(i):
            ip = i % 2
            ln3d = stage2_in[i]
            p.wait("tensor", [ln3d, st["pTc"]])
            evs = []
            for half_k in range(2):
                tr = None
                for k8 in range(8):
                    k = half_k * 8 + k8
                    tr = p.inc("tensor", lambda e, k=k, k8=k8, ip=ip: e.transpose(
                        pTc[:, k8 * 128:(k8 + 1) * 128],
                        ln3_bf[ip][:, k * 128:(k + 1) * 128], idb[:]), "tt")
                p.wait("vector", [tr, st["ln3Tt"][ip]])
                evs.append(p.inc("vector", lambda e, half_k=half_k, ip=ip: e.tensor_copy(
                    ln3Tt[ip][:, half_k * 1024:(half_k + 1) * 1024], pTc[:]), "vv"))
                if half_k == 0:
                    p.wait("tensor", evs[-1])
            st["pTc"] = evs[-1]
            st["ln3bf"][ip] = [evs[-1], st["lxd"][ip]]
            sp = []
            for k in range(KC):
                p.wait("sync", evs[k // 8])
                sp.append(p.inc("sync", lambda e, k=k, i=i, ip=ip: e.dma_start(
                    ln3T_sp[k][:, i * 128:(i + 1) * 128],
                    ln3Tt[ip][:, k * 128:(k + 1) * 128]), f"l3s{ip}", 16))
            p.wait("tensor", [evs[-1], gld, st["pG"]])
            gm = None
            for k in range(KC):
                gm = p.inc("tensor", lambda e, k=k, ip=ip: e.matmul(
                    pG[:], ln3Tt[ip][:, k * 128:(k + 1) * 128],
                    gate_sb[:, k * NE:(k + 1) * NE],
                    start=(k == 0), stop=(k == KC - 1)), "tt")
            st["ln3Tt"][ip] = [gm] + sp
            p.wait("scalar", gm)
            sg = p.inc("scalar", lambda e: e.activation(s_sb[:], pG[:], ACT.Sigmoid), "ss")
            st["pG"] = sg
            p.wait("vector", [sg, bld])
            p.drain("vector")
            p.op("vector", lambda e: e.tensor_add(sbb[:], s_sb[:], bias_sb[:]))
            p.drain("vector")
            p.op("vector", lambda e: e.reduce_max(st_w[:, 0:1], sbb[:], axis=AX))
            p.drain("vector")
            p.op("vector", lambda e: e.tensor_scalar(
                eqm[:], sbb[:], st_w[:, 0:1], None, OP.is_equal))
            p.drain("vector")
            p.op("vector", lambda e: e.scalar_tensor_tensor(
                eqm[:], eqm[:], NEG, sbb[:], OP.mult, OP.add))
            p.drain("vector")
            p.op("vector", lambda e: e.reduce_max(st_w[:, 0:1], eqm[:], axis=AX))
            p.drain("vector")
            p.op("vector", lambda e: e.tensor_scalar_add(st_w[:, 2:3], st_w[:, 0:1], -GDELTA))
            p.drain("vector")
            p.op("vector", lambda e: e.tensor_scalar(
                gse[:], sbb[:], st_w[:, 2:3], None, OP.is_ge))
            p.drain("vector")
            p.op("vector", lambda e: e.tensor_scalar(
                eqm[:], sbb[:], st_w[:, 0:1], None, OP.is_ge))
            p.drain("vector")
            p.op("vector", lambda e: e.tensor_mul(wsel[:], s_sb[:], eqm[:]))
            p.drain("vector")
            p.op("vector", lambda e: e.reduce_sum(st_w[:, 0:1], wsel[:], axis=AX))
            p.drain("vector")
            p.op("vector", lambda e: e.reciprocal(st_w[:, 1:2], st_w[:, 0:1]))
            p.drain("vector")
            wed = p.inc("vector", lambda e: e.tensor_scalar(
                wsel[:], wsel[:], st_w[:, 1:2], None, OP.mult), "vv")
            p.wait("vector", [st["wex"], wem])
            p.drain("vector")
            p.op("vector", lambda e: e.tensor_copy(we_ext[:, 0:8], wsel[:]))
            p.drain("vector")
            wxd = p.inc("vector", lambda e, i=i: e.memset(we_ext[:, 8:9], float(i)), "vv")
            p.wait("sync", wxd)
            st["wex"] = p.inc("sync", lambda e, i=i: e.dma_start(
                lx_sp[i][:, 2048:2176], we_ext[:]), "lxe", 16)
            p.wait("tensor", [wed, st["pX"]])
            p.inc("tensor", lambda e: e.transpose(
                pX[0:NE, 0:128], wsel[:], idf[:]), "tt")
            wt = p.inc("tensor", lambda e: e.transpose(
                pX[0:NE, 128:256], gse[:], idf[:]), "tt")
            p.wait("vector", wt)
            p.op("vector", lambda e, i=i: e.tensor_copy(
                weT[:, i * 128:(i + 1) * 128], pX[0:NE, 0:128]))
            st["pX"] = p.inc("vector", lambda e, i=i: e.tensor_copy(
                gweT[:, i * 128:(i + 1) * 128], pX[0:NE, 128:256]), "vv")

        for i in range(NQT + 1):
            if i < NQT:
                stage1(i)
            if i >= 1:
                stage2(i - 1)
        p.quiesce("sync")
        p.run()

    ECAP = 1024
    with contextlib.ExitStack() as ps:
        def sb(name, shape, dt):
            return ps.enter_context(nc.sbuf_tensor(name, shape, dt))
        ln3h = sb("ln3h", [128, KC * 1024], BF16)
        h_sb = sb("h_sb", [128, 11 * 1024], BF16)
        dn_sb = sb("dn_sb", [128, 11 * 2048], BF16)
        upg = [sb(f"upg{i}", [128, C], BF16) for i in range(2)]
        upu = [sb(f"upu{i}", [128, C], BF16) for i in range(2)]
        lnx = sb("lnx", [128, 4 * 2176], BF16)
        lnx_fm = sb("lnx_fm", [128, KC * 1024], BF16)
        stage = [sb(f"stage{i}", [128, C], F32) for i in range(2)]
        cumC = sb("cumC", [8, TLOC], F32)[:]
        selA = stage[0][0:8, :]
        onesB = stage[1][0:8, :]
        slots_i = sb("slots_i", [128, 128], mybir.dt.int32)
        wecol = sb("wecol", [128, 8], F32)
        tokf = sb("tokf", [128, 8], F32)
        toki = sb("toki", [128, 8], mybir.dt.int32)
        sg_bf = sb("sg_bf", [128, 512], BF16)
        uw_bf = sb("uw_bf", [128, 512], BF16)
        st_d = sb("st_d", [128, 8], F32)
        pGg = ps.enter_context(nc.psum_tensor("pGg", [128, 512], F32))
        pGu = ps.enter_context(nc.psum_tensor("pGu", [128, 512], F32))
        pD = ps.enter_context(nc.psum_tensor("pD", [128, 2048], F32))
        pTe = ps.enter_context(nc.psum_tensor("pTe", [128, 1024], BF16))
        pX2 = ps.enter_context(nc.psum_tensor("pX2", [128, 512], F32))

        upg_free = [None, None]
        upu_free = [None, None]
        dn_free = None
        h_free = [None] * 11
        pGg_free = None
        pGu_free = None
        pD_free = None
        pTe_free = None
        sg_free = None
        uw_free = None
        acc_chain = None
        lnx_free = None
        scat_done = None
        sli = None

        def emit_branch(p, kind, be, hf, lnx_ready, dn_srcs):
            nonlocal dn_free, pGg_free, pGu_free, pD_free, sg_free, uw_free
            nonlocal acc_chain
            ngrp = 2
            for j in range(11):
                jp = j % 2
                if kind == "sha":
                    mg, mu = 22 + j, j
                elif kind == "shb":
                    mg, mu = 33 + j, 11 + j
                else:
                    mg, mu = j, 11 + j
                p.wait("gpsimd", upg_free[jp])
                if kind == "e":
                    gl = p.inc("gpsimd", lambda e, be=be, mg=mg, jp=jp: e.dma_start(
                        upg[jp][:], up_l[be, mg]), f"ug{jp}", 16)
                else:
                    gl = p.inc("gpsimd", lambda e, mg=mg, jp=jp: e.dma_start(
                        upg[jp][:], gw_l[mg]), f"ug{jp}", 16)
                p.wait("gpsimd", upu_free[jp])
                if kind == "e":
                    ul = p.inc("gpsimd", lambda e, be=be, mu=mu, jp=jp: e.dma_start(
                        upu[jp][:], up_l[be, mu]), f"uu{jp}", 16)
                else:
                    ul = p.inc("gpsimd", lambda e, mu=mu, jp=jp: e.dma_start(
                        upu[jp][:], gw_l[mu]), f"uu{jp}", 16)
                for n2 in range(ngrp):
                    def rhs_ap(k, n2=n2, kind=kind):
                        if kind == "e":
                            return lnx_fm[:, k * 1024 + n2 * 512: k * 1024 + (n2 + 1) * 512]
                        return ln3h[:, k * 1024 + n2 * 512: k * 1024 + (n2 + 1) * 512]
                    p.wait("tensor", [gl, pGg_free] + ([lnx_ready] if n2 == 0 else []))
                    mmg = None
                    for k in range(KC):
                        mmg = p.inc("tensor", lambda e, k=k, jp=jp, rhs_ap=rhs_ap: e.matmul(
                            pGg[:], upg[jp][:, k * 128:(k + 1) * 128], rhs_ap(k),
                            start=(k == 0), stop=(k == KC - 1)), "tt")
                    p.wait("tensor", [ul, pGu_free])
                    mmu = None
                    for k in range(KC):
                        mmu = p.inc("tensor", lambda e, k=k, jp=jp, rhs_ap=rhs_ap: e.matmul(
                            pGu[:], upu[jp][:, k * 128:(k + 1) * 128], rhs_ap(k),
                            start=(k == 0), stop=(k == KC - 1)), "tt")
                    if n2 == ngrp - 1:
                        upg_free[jp] = mmg
                        upu_free[jp] = mmu
                    p.wait("scalar", [mmg, sg_free])
                    sgd = p.inc("scalar", lambda e: e.activation(
                        sg_bf[:], pGg[:], ACT.Silu), "ss")
                    pGg_free = sgd
                    p.wait("vector", [mmu, uw_free])
                    p.drain("vector")
                    uwd = p.inc("vector", lambda e: e.tensor_copy(uw_bf[:], pGu[:]), "vv")
                    pGu_free = uwd
                    p.wait("vector", [sgd, h_free[j]])
                    p.drain("vector")
                    hd = p.inc("vector", lambda e, j=j, n2=n2: e.tensor_mul(
                        h_sb[:, j * 1024 + n2 * 512: j * 1024 + (n2 + 1) * 512],
                        sg_bf[:], uw_bf[:]), "vv")
                    sg_free = hd
                    uw_free = hd
                    h_free[j] = hd
                p.wait("gpsimd", dn_free)
                dl = p.inc("gpsimd", lambda e, j=j, src=dn_srcs[j]: e.dma_start(
                    dn_sb[:, j * 2048:(j + 1) * 2048], src), "dnl", 16)
                if j == 10:
                    p._dn_tok = dl
            ntile = 8
            for qt in range(ntile):
                sp_ = qt % 2
                p.wait("tensor", [pD_free] +
                       (([h_free[jj] for jj in range(11)] + [p._dn_tok]) if qt == 0 else []))
                mm = None
                for j in range(11):
                    for n in range(4):
                        mm = p.inc("tensor", lambda e, j=j, n=n, qt=qt: e.matmul(
                            pD[:, n * 512:(n + 1) * 512],
                            h_sb[:, j * 1024 + qt * 128: j * 1024 + qt * 128 + 128],
                            dn_sb[:, j * 2048 + n * 512: j * 2048 + (n + 1) * 512],
                            start=(j == 0), stop=(j == 10)), "tt")
                p.wait("vector", [mm, acc_chain] + ([None] if kind != "e" else []))
                p.drain("vector")
                if kind == "e":
                    ev = p.inc("vector", lambda e, sp_=sp_, qt=qt: e.tensor_scalar(
                        stage[sp_][:], pD[:], wecol[:, qt:qt + 1], None, OP.mult), "vv")
                else:
                    ev = p.inc("vector", lambda e, sp_=sp_: e.tensor_copy(
                        stage[sp_][:], pD[:]), "vv")
                pD_free = ev
                if kind == "sha":
                    p.wait("sync", [ev, acc_chain])
                    acc_chain = p.inc("sync", lambda e, hf=hf, qt=qt, sp_=sp_: e.dma_start(
                        acc_sp[(hf * 8 + qt) * 128:(hf * 8 + qt + 1) * 128, :],
                        stage[sp_][:]), "aci", 16)
                elif kind == "shb":
                    p.wait("gpsimd", [ev, acc_chain])
                    acc_chain = p.inc("gpsimd", lambda e, hf=hf, qt=qt, sp_=sp_: e.dma_start(
                        acc_sp[(hf * 8 + qt) * 128:(hf * 8 + qt + 1) * 128, :],
                        stage[sp_][:], accum_op=OP.add), "acs", 16)
                else:
                    p.wait("gpsimd", [ev, acc_chain, p._tokd])
                    acc_chain = p.inc("gpsimd", lambda e, qt=qt, sp_=sp_: e.indirect_dma_start(
                        acc_sp[:],
                        bass.IndirectOffsetOnAxis(toki[:, qt:qt + 1], 0),
                        stage[sp_][:], None, compute_op=OP.add), "acs", 16)
            mm_last = p.tok("tt")
            dn_free = mm_last
            for jj in range(11):
                h_free[jj] = mm_last

        p = _Ph(nc, sems, cnt)
        prev_tt = p.tok("tt")
        lnld = None
        for k in range(KC):
            if k == 0:
                p.wait("sync", prev_tt)
            lnld = p.inc("sync", lambda e, k=k: e.dma_start(
                ln3h[:, k * 1024:(k + 1) * 1024], ln3T_sp[k][:, 0:1024]), "l3h", 16)
        p.op("vector", lambda e: e.tensor_single_scalar(
            selA, gweT[:], 0.5, OP.is_gt))
        p.drain("vector")
        p.op("vector", lambda e: e.memset(onesB, 1.0))
        p.drain("vector")
        vscan = p.inc("vector", lambda e: e.tensor_tensor_scan(
            cumC, onesB, selA, 0.0, OP.mult, OP.add), "vv")
        p.wait("gpsimd", vscan)
        iog = p.inc("gpsimd", lambda e: e.iota(
            onesB, pattern=[[1, TLOC]], base=0, channel_multiplier=0,
            allow_small_or_imprecise_dtypes=True), "gpc")
        p.wait("vector", iog)
        p.drain("vector")
        p.op("vector", lambda e: e.tensor_sub(onesB, onesB, cumC))
        p.drain("vector")
        p.op("vector", lambda e: e.tensor_scalar_add(
            onesB, onesB, cumC[:, TLOC - 1:TLOC]))
        p.drain("vector")
        p.op("vector", lambda e: e.tensor_scalar_add(cumC, cumC, -1.0))
        p.drain("vector")
        p.op("vector", lambda e: e.tensor_sub(cumC, cumC, onesB))
        p.drain("vector")
        p.op("vector", lambda e: e.tensor_mul(selA, selA, cumC))
        p.drain("vector")
        sld = p.inc("vector", lambda e: e.tensor_add(onesB, onesB, selA), "vv")
        cp_tok = None
        for rnd in range(2):
            p.wait("tensor", [sld, cp_tok])
            trs = None
            for c8 in range(8):
                c = rnd * 8 + c8
                trs = p.inc("tensor", lambda e, c=c, c8=c8: e.transpose(
                    pX2[:, c8 * 8: c8 * 8 + 8],
                    onesB[:, c * 128:(c + 1) * 128], idf[0:8, 0:8]), "tt")
            p.wait("vector", trs)
            p.drain("vector")
            cp_tok = p.inc("vector", lambda e, rnd=rnd: e.tensor_copy(
                slots_i[:, rnd * 64: rnd * 64 + 64], pX2[:, 0:64]), "vv")
        sli = cp_tok
        scat_tok = [None, None]
        for cch in range(KC):
            cp = cch % 2
            p.wait("scalar", scat_tok[cp])
            lr = p.inc("scalar", lambda e, cch=cch, cp=cp: e.dma_start(
                lnx[:, cp * 2176:(cp + 1) * 2176], lx_sp[cch]), f"lxr{cp}", 16)
            p.wait("gpsimd", [lr, sli])
            last = None
            for e2 in range(NE):
                last = p.inc("gpsimd", lambda e, e2=e2, cch=cch, cp=cp:
                             e.indirect_dma_start(
                    lxe_sp[e2],
                    bass.IndirectOffsetOnAxis(
                        slots_i[:, cch * 8 + e2: cch * 8 + e2 + 1], 0),
                    lnx[:, cp * 2176:(cp + 1) * 2176], None), f"fs{cp}", 16)
            scat_tok[cp] = last
        scat_done = [scat_tok[0], scat_tok[1]]
        dn_sha = [dwT[j * 128:(j + 1) * 128, :] for j in range(11)]
        dn_shb = [dwT[(11 + j) * 128:(12 + j) * 128, :] for j in range(11)]
        emit_branch(p, "sha", 0, 0, lnld, dn_sha)
        emit_branch(p, "shb", 0, 0, lnld, dn_shb)
        p.quiesce("sync")
        p.run()

        p = _Ph(nc, sems, cnt)
        lnld = None
        for k in range(KC):
            lnld = p.inc("sync", lambda e, k=k: e.dma_start(
                ln3h[:, k * 1024:(k + 1) * 1024], ln3T_sp[k][:, 1024:2048]), "l3h", 16)
        emit_branch(p, "sha", 0, 1, lnld, dn_sha)
        emit_branch(p, "shb", 0, 1, lnld, dn_shb)
        p.quiesce("sync")
        p.run()

        p = _Ph(nc, sems, cnt)
        for be in range(NE):
            dn_e = [dnT[be, j * 128:(j + 1) * 128, :] for j in range(11)]
            tokd = None
            lnx_ready = None
            for sg2 in range(2):
                p.wait("scalar", [scat_done, lnx_free])
                lxl = None
                for cch in range(4):
                    lxl = p.inc("scalar", lambda e, be=be, sg2=sg2, cch=cch: e.dma_start(
                        lnx[:, cch * 2176:(cch + 1) * 2176],
                        lxe_sp[be][sg2 * 512 + cch * 128: sg2 * 512 + (cch + 1) * 128, :]),
                        "lnx", 16)
                p.wait("vector", [lxl, acc_chain, tokd])
                p.drain("vector")
                for cch in range(4):
                    qt8 = sg2 * 4 + cch
                    p.op("vector", lambda e, cch=cch, be=be, qt8=qt8: e.tensor_copy(
                        wecol[:, qt8:qt8 + 1],
                        lnx[:, cch * 2176 + 2048 + be: cch * 2176 + 2048 + be + 1]))
                    p.op("vector", lambda e, cch=cch, qt8=qt8: e.scalar_tensor_tensor(
                        tokf[:, qt8:qt8 + 1],
                        lnx[:, cch * 2176 + 2048 + 8: cch * 2176 + 2048 + 9],
                        128.0,
                        lnx[:, cch * 2176 + 2048 + 9: cch * 2176 + 2048 + 10],
                        OP.mult, OP.add))
                p.drain("vector")
                tokd = p.inc("vector", lambda e, sg2=sg2: e.tensor_copy(
                    toki[:, sg2 * 4:(sg2 + 1) * 4], tokf[:, sg2 * 4:(sg2 + 1) * 4]), "vv")
                tcnt = 0
                evs = []
                for cch in range(4):
                    for k in range(KC):
                        if tcnt % 8 == 0:
                            p.wait("tensor", [pTe_free, lxl])
                        tr = p.inc("tensor", lambda e, cch=cch, k=k, tcnt=tcnt: e.transpose(
                            pTe[:, (tcnt % 8) * 128:(tcnt % 8 + 1) * 128],
                            lnx[:, cch * 2176 + k * 128: cch * 2176 + (k + 1) * 128],
                            idb[:]), "tt")
                        tcnt += 1
                        if tcnt % 8 == 0:
                            kk0 = k - 7
                            p.wait("vector", tr)
                            ev = p.inc("vector", lambda e, cch=cch, kk0=kk0, sg2=sg2:
                                       e.tensor_copy(
                                lnx_fm.rearrange("p (k c) -> p k c", k=KC)
                                [:, kk0:kk0 + 8, sg2 * 512 + cch * 128:
                                 sg2 * 512 + (cch + 1) * 128],
                                pTe.rearrange("p (k c) -> p k c", k=8)[:, :, :]), "vv")
                            evs.append(ev)
                            pTe_free = ev
                lnx_ready = [lnx_ready, evs[-1]]
                lnx_free = evs[-1]
            p._tokd = tokd
            emit_branch(p, "e", be, 0, lnx_ready, dn_e)
        x3_prev = None
        xl_prev = None
        for i in range(NQT):
            p.wait("sync", [xl_prev, x3_prev, acc_chain])
            al = p.inc("sync", lambda e, i=i: e.dma_start(
                stage[1][:], acc_sp[i * 128:(i + 1) * 128, :]), "acl", 16)
            xld = p.inc("sync", lambda e, i=i: e.dma_start(
                stage[0][:], x2_sp[i]), "x2r", 16)
            p.wait("vector", al)
            p.drain("vector")
            p.inc("vector", lambda e: e.reduce_sum(st_d[:, 0:1], stage[1][:], axis=AX), "vv")
            p.wait("scalar", [al, pD_free])
            sq1 = p.inc("scalar", lambda e: e.activation(
                pD[:], stage[1][:], ACT.Square, accum_out=st_d[:, 1:2]), "ss")
            pD_free = sq1
            rstd = _ln_stats(p, st_d, [p.tok("vv"), sq1])
            p.wait("vector", [rstd, xld])
            p.drain("vector")
            p.op("vector", lambda e: e.tensor_scalar(
                stage[1][:], stage[1][:], st_d[:, 0:1], st_d[:, 5:6],
                OP.subtract, OP.mult))
            p.drain("vector")
            x3d = p.inc("vector", lambda e: e.tensor_add(
                stage[0][:], stage[0][:], stage[1][:]), "vv")
            xl_prev = x3d
            p.wait("sync", x3d)
            x3_prev = p.inc("sync", lambda e, i=i: e.dma_start(
                yout[i * 128:(i + 1) * 128, :], stage[0][:]), "out", 16)
        p.quiesce("sync")
        p.run()

    return nc


def kernel(**inputs):
    shared = _prep_shared(inputs)
    in_maps = []
    for core in range(8):
        b, half = core // 2, core % 2
        in_maps.append(_prep_core(inputs, shared, b, half))
    nc = _build_nc()
    res = run_bass_kernel_spmd(nc, in_maps, core_ids=list(range(8)))
    out = np.empty((B, T, C), dtype=np.float32)
    for core in range(8):
        b, half = core // 2, core % 2
        out[b, half * TLOC:(half + 1) * TLOC] = res.results[core]["yout"]
    return out
